# version 01 (final)
# baseline (speedup 1.0000x reference)
import numpy as np
import ml_dtypes
import concourse.bass as bass
import concourse.mybir as mybir
from concourse.bass_utils import run_bass_kernel_spmd
from contextlib import ExitStack

F32 = mybir.dt.float32
BF16 = mybir.dt.bfloat16
I16 = mybir.dt.int16
I32 = mybir.dt.int32
AF = mybir.ActivationFunctionType
ALU = mybir.AluOpType
AX = mybir.AxisListType
IOA = bass.IndirectOffsetOnAxis
BF = ml_dtypes.bfloat16

D = 2048
NT = 1024
KC = 16
HID0 = 5632
OC0 = 44
HID1 = 8192
OC1 = 64
EPS_LN = 1e-5
TWO_PI = 6.283185307179586



class Sem:
    __slots__ = ("h", "n", "_cm")

    def __init__(self, nc, name):
        self._cm = nc.semaphore(name)
        self.h = self._cm.__enter__()
        self.n = 0


class Prog:

    ENGS = ("sync", "vector", "scalar", "gpsimd", "tensor")

    def __init__(self, nc):
        self.nc = nc
        self.ops = {e: [] for e in self.ENGS}
        self.sems = {}

    SELF_SYNC = ("vector", "scalar", "gpsimd")

    def sem(self, name):
        if name not in self.sems:
            self.sems[name] = Sem(self.nc, name)
        return self.sems[name]

    def esem(self, eng):
        return self.sem("E_" + eng)

    def wait(self, eng, sem, n):
        if n > 0:
            self.ops[eng].append(lambda e, s=sem, n=n: e.wait_ge(s.h, n))

    def op(self, eng, fn, inc=None, k=1):
        if inc is None:
            self.ops[eng].append(fn)
            return None
        if eng in self.SELF_SYNC and inc is self.sems.get("E_" + eng) and inc.n > 0:
            self.ops[eng].append(lambda e, s=inc, n=inc.n: e.wait_ge(s.h, n))
        tag = inc.n + k
        inc.n = tag

        def wrapped(e, fn=fn, inc=inc, k=k):
            fn(e).then_inc(inc.h, k)

        self.ops[eng].append(wrapped)
        return tag

    def raw(self, eng, fn):
        self.ops[eng].append(fn)

    def join(self, name, engines):
        s = self.sem("J_" + name)
        target = s.n + len(engines)
        s.n = target
        for e in engines:
            self.ops[e].append(lambda en, s=s: en.nop().then_inc(s.h, 1))
        for e in engines:
            self.wait(e, s, target)

    def emit(self, block):
        for e in self.ENGS:
            ops = self.ops[e]
            if not ops:
                continue

            def body(en, ops=ops):
                for f in ops:
                    f(en)

            getattr(block, e)(body)


def ceil_div(a, b):
    return (a + b - 1) // b


class MK:
    def __init__(self):
        self.nc = bass.Bass("TRN2", target_bir_lowering=False, debug=False)
        self.ctx = ExitStack()
        self.p = Prog(self.nc)
        self.dram = {}
        self.sb = {}
        self.ps = {}

    def dram_in(self, name, shape, dtype=F32):
        self.dram[name] = self.nc.dram_tensor(name, list(shape), dtype,
                                              kind="ExternalInput").ap()
        return self.dram[name]

    def dram_out(self, name, shape, dtype=F32):
        self.dram[name] = self.nc.dram_tensor(name, list(shape), dtype,
                                              kind="ExternalOutput").ap()
        return self.dram[name]

    def dram_tmp(self, name, shape, dtype=F32, shared=False):
        kw = {"addr_space": "Shared"} if shared else {}
        self.dram[name] = self.nc.dram_tensor(name, list(shape), dtype,
                                              kind="Internal", **kw).ap()
        return self.dram[name]

    def sbuf(self, name, shape, dtype=F32):
        h = self.ctx.enter_context(self.nc.sbuf_tensor(name, list(shape), dtype))
        self.sb[name] = h
        return h

    def psum(self, name, shape, dtype=F32):
        h = self.ctx.enter_context(self.nc.psum_tensor(name, list(shape), dtype))
        self.ps[name] = h
        return h

    def finish(self):
        with self.nc.Block() as block:
            self.p.emit(block)
        self.ctx.close()
        return self.nc


def alloc_consts(mk):
    mk.sbuf("ONES1", [1, 128], F32)
    mk.sbuf("ONE1B", [1, 128], BF16)
    mk.sbuf("ONESB", [128, 1], BF16)
    mk.sbuf("EPS1", [1, 1], F32)
    mk.sbuf("M128B", [1, 128], BF16)
    p = mk.p
    g0 = p.sem("cinit")
    p.op("gpsimd", lambda e: e.memset(mk.sb["ONES1"][:], 1.0), inc=g0)
    p.op("gpsimd", lambda e: e.memset(mk.sb["ONE1B"][:], 1.0), inc=g0)
    p.op("gpsimd", lambda e: e.memset(mk.sb["ONESB"][:], 1.0), inc=g0)
    p.op("gpsimd", lambda e: e.memset(mk.sb["EPS1"][:], EPS_LN), inc=g0)
    p.op("gpsimd", lambda e: e.memset(mk.sb["M128B"][:], -128.0), inc=g0)
    for e in ("tensor", "vector", "scalar", "sync"):
        p.wait(e, g0, g0.n)



def emit_ln(mk, xin_t, dst, ntok, name):
    p, sb = mk.p, mk.sb
    XT, RS, SCR = sb["XT"], sb["RS"], sb["SCR"]
    PL = mk.ps["PL"]
    s_x = p.sem(name + "x")
    s_g = p.esem("gpsimd")
    s_t = p.esem("tensor")
    s_v = p.esem("vector")
    s_s = p.esem("scalar")
    ntile = ceil_div(ntok, 512)
    RA = RS[0:1, 1024:1536]
    RB = RS[0:1, 1536:2048]
    RC = RS[0:1, 2048:2560]
    RSTD_B = RS[:, 0:512]
    MR_B = RS[:, 512:1024]

    xt_consumed = {}
    for ti in range(ntile):
        tw = min(512, ntok - ti * 512)
        for c in range(KC):
            par = c % 2
            if par in xt_consumed:
                p.wait("sync", s_g, xt_consumed[par])
            xt = p.op("sync", lambda e, c=c, ti=ti, tw=tw, par=par:
                      e.dma_start(out=XT[:, par * 512: par * 512 + tw],
                                  in_=xin_t[c, :, ti * 512: ti * 512 + tw]),
                      inc=s_x, k=16)
            p.wait("gpsimd", s_x, xt)
            t1 = p.op("gpsimd", lambda e, par=par, ti=ti, tw=tw:
                      e.tensor_copy(SCR[:, 0:tw],
                                    XT[:, par * 512: par * 512 + tw]), inc=s_g)
            xt_consumed[par] = t1
            t2 = p.op("gpsimd", lambda e, tw=tw:
                      e.tensor_tensor(SCR[:, 512:512 + tw], SCR[:, 0:tw],
                                      SCR[:, 0:tw], ALU.mult), inc=s_g)
            p.wait("tensor", s_g, t2)
            p.op("tensor", lambda e, c=c, tw=tw:
                 e.matmul(PL[0:1, 0:tw], sb["ONESB"][:, 0:1], SCR[:, 0:tw],
                          start=(c == 0), stop=(c == KC - 1)), inc=s_t)
            mt = p.op("tensor", lambda e, c=c, tw=tw:
                      e.matmul(PL[0:1, 512:512 + tw], sb["ONESB"][:, 0:1],
                               SCR[:, 512:512 + tw],
                               start=(c == 0), stop=(c == KC - 1)), inc=s_t)
            p.wait("gpsimd", s_t, mt)
        p.wait("vector", s_t, s_t.n)
        p.op("vector", lambda e, tw=tw:
             e.tensor_scalar(RA[:, 0:tw], PL[0:1, 0:tw], 1.0 / D, None, ALU.mult),
             inc=s_v)
        p.op("vector", lambda e, tw=tw:
             e.tensor_scalar(RB[:, 0:tw], PL[0:1, 512:512 + tw], 1.0 / D, None,
                             ALU.mult), inc=s_v)
        p.op("vector", lambda e, tw=tw:
             e.tensor_tensor(RC[:, 0:tw], RA[:, 0:tw], RA[:, 0:tw], ALU.mult),
             inc=s_v)
        tv = p.op("vector", lambda e, tw=tw:
                  e.tensor_tensor(RB[:, 0:tw], RB[:, 0:tw], RC[:, 0:tw],
                                  ALU.subtract), inc=s_v)
        p.wait("scalar", s_v, tv)
        ts_ = p.op("scalar", lambda e, tw=tw:
                   e.activation(RC[:, 0:tw], RB[:, 0:tw], AF.Sqrt,
                                bias=sb["EPS1"][0:1, 0:1], scale=1.0), inc=s_s)
        p.wait("vector", s_s, ts_)
        p.op("vector", lambda e, tw=tw:
             e.reciprocal(RB[:, 0:tw], RC[:, 0:tw]), inc=s_v)
        tv = p.op("vector", lambda e, tw=tw:
                  e.tensor_tensor(RC[:, 0:tw], RA[:, 0:tw], RB[:, 0:tw],
                                  ALU.mult), inc=s_v)
        p.wait("tensor", s_v, tv)
        p.op("tensor", lambda e, tw=tw:
             e.matmul(PL[:, 1024:1024 + tw], sb["ONES1"][0:1, :], RB[:, 0:tw],
                      start=True, stop=True), inc=s_t)
        mt = p.op("tensor", lambda e, tw=tw:
                  e.matmul(PL[:, 1536:1536 + tw], sb["ONES1"][0:1, :], RC[:, 0:tw],
                           start=True, stop=True), inc=s_t)
        p.wait("vector", s_t, mt)
        p.op("vector", lambda e, tw=tw:
             e.tensor_copy(RSTD_B[:, 0:tw], PL[:, 1024:1024 + tw]), inc=s_v)
        tv = p.op("vector", lambda e, tw=tw:
                  e.tensor_copy(MR_B[:, 0:tw], PL[:, 1536:1536 + tw]), inc=s_v)
        p.wait("gpsimd", s_v, tv)
        for c in range(KC):
            par = c % 2
            if par in xt_consumed:
                p.wait("sync", s_g, xt_consumed[par])
            xt = p.op("sync", lambda e, c=c, ti=ti, tw=tw, par=par:
                      e.dma_start(out=XT[:, par * 512: par * 512 + tw],
                                  in_=xin_t[c, :, ti * 512: ti * 512 + tw]),
                      inc=s_x, k=16)
            p.wait("gpsimd", s_x, xt)
            tg = p.op("gpsimd", lambda e, par=par, ti=ti, tw=tw:
                      e.tensor_tensor(RS[:, 2560:2560 + tw],
                                      XT[:, par * 512: par * 512 + tw],
                                      RSTD_B[:, 0:tw], ALU.mult), inc=s_g)
            xt_consumed[par] = tg
            p.wait("vector", s_g, tg)
            tv = p.op("vector", lambda e, c=c, ti=ti, tw=tw:
                      e.tensor_tensor(dst[:, c * ntok + ti * 512:
                                          c * ntok + ti * 512 + tw],
                                      RS[:, 2560:2560 + tw], MR_B[:, 0:tw],
                                      ALU.subtract), inc=s_v)
            p.wait("gpsimd", s_v, tv)
        p.wait("tensor", s_v, s_v.n)
    p.wait("sync", p.esem("gpsimd"), p.esem("gpsimd").n)



def emit_colsum(mk, src_slice_fn, nchunks, row_off, name, s_t, s_v, ppsum):
    p, sb = mk.p, mk.sb
    SGB = sb["SGB"]
    for ti in range(2):
        for c in range(nchunks):
            p.op("tensor", lambda e, c=c, ti=ti:
                 e.matmul(ppsum[0:1, ti * 512:(ti + 1) * 512], sb["ONESB"][:, 0:1],
                          src_slice_fn(c, ti * 512, (ti + 1) * 512),
                          start=(c == 0), stop=(c == nchunks - 1)), inc=s_t)
    p.wait("vector", s_t, s_t.n)
    p.op("vector", lambda e, row_off=row_off:
         e.tensor_copy(SGB[0:1, row_off:row_off + 1024], ppsum[0:1, 0:1024]),
         inc=s_v)
    tag = p.op("vector", lambda e, row_off=row_off:
               e.tensor_tensor(SGB[0:1, row_off + 1024:row_off + 2048],
                               ppsum[0:1, 0:1024],
                               SGB[0:1, row_off:row_off + 1024], ALU.subtract),
               inc=s_v)
    p.wait("tensor", s_v, tag)


def emit_zp_fix(mk, dst, bo, row_off, ti, s_t, stop):
    p, sb = mk.p, mk.sb
    SGB = sb["SGB"]
    p.op("tensor", lambda e:
         e.matmul(dst[:, bo:bo + 512], sb["M128B"][0:1, :],
                  SGB[0:1, row_off + ti * 512: row_off + ti * 512 + 512],
                  start=False, stop=False), inc=s_t)
    p.op("tensor", lambda e:
         e.matmul(dst[:, bo:bo + 512], sb["M128B"][0:1, :],
                  SGB[0:1, row_off + 1024 + ti * 512: row_off + 1024 + ti * 512 + 512],
                  start=False, stop=stop), inc=s_t)


def emit_ffn0(mk, xin_t, xout_t, name):
    p, sb = mk.p, mk.sb
    wg, wu, wd = mk.dram["wg0"], mk.dram["wu0"], mk.dram["wd0"]
    sgt, sut, sdt = sb["SG0"], sb["SU0"], sb["SD0"]
    H, SCR, XT = sb["HB"], sb["SCR"], sb["XT"]
    WRG, WRU = sb["WRG"], sb["WRU"]
    pG, pU = mk.ps["PL"], mk.ps["PM"]
    s_w = p.sem(name + "w")
    s_t = p.esem("tensor")
    s_v = p.esem("vector")
    s_s = p.esem("scalar")
    s_x = p.sem(name + "x")

    emit_colsum(mk, lambda c, a, b: H[:, c * NT + a: c * NT + b], KC, 0,
                name + "csh", s_t, s_v, pG)

    war = {}
    for og in range(OC0 // 2):
        oc0 = og * 2
        wtag_g = p.op("gpsimd", lambda e, og=og:
                      e.dma_start(out=WRG[:], in_=wg[og]), inc=s_w, k=16)
        wtag_u = p.op("gpsimd", lambda e, og=og:
                      e.dma_start(out=WRU[:], in_=wu[og]), inc=s_w, k=16)
        p.wait("tensor", s_w, wtag_u)
        for kc in range(KC):
            for ti in range(2):
                for j in range(2):
                    bank = j * 2 + ti
                    p.op("tensor", lambda e, kc=kc, j=j, ti=ti, bank=bank:
                         e.matmul(pG[:, bank * 512:(bank + 1) * 512],
                                  WRG[:, kc * 256 + j * 128: kc * 256 + (j + 1) * 128],
                                  H[:, kc * NT + ti * 512: kc * NT + (ti + 1) * 512],
                                  start=(kc == 0), stop=False), inc=s_t)
                    p.op("tensor", lambda e, kc=kc, j=j, ti=ti, bank=bank:
                         e.matmul(pU[:, bank * 512:(bank + 1) * 512],
                                  WRU[:, kc * 256 + j * 128: kc * 256 + (j + 1) * 128],
                                  H[:, kc * NT + ti * 512: kc * NT + (ti + 1) * 512],
                                  start=(kc == 0), stop=False), inc=s_t)
        for ti in range(2):
            for j in range(2):
                bank = j * 2 + ti
                emit_zp_fix(mk, pG, bank * 512, 0, ti, s_t, True)
                emit_zp_fix(mk, pU, bank * 512, 0, ti, s_t, True)
        mmdone = s_t.n
        p.wait("scalar", s_t, mmdone)
        p.wait("vector", s_t, mmdone)
        for j in range(2):
            oc = oc0 + j
            for ti in range(2):
                bank = j * 2 + ti
                if ("s", ti) in war:
                    p.wait("scalar", s_v, war[("s", ti)])
                ts_ = p.op("scalar", lambda e, bank=bank, oc=oc, ti=ti:
                           e.activation(SCR[:, ti * 512:(ti + 1) * 512],
                                        pG[:, bank * 512:(bank + 1) * 512],
                                        AF.Silu,
                                        scale=sgt[:, oc:oc + 1]), inc=s_s)
                p.wait("vector", s_s, ts_)
                dsl = g_slice(mk, oc, ti * 512, (ti + 1) * 512)
                war[("s", ti)] = p.op("vector",
                     lambda e, bank=bank, oc=oc, dsl=dsl, ti=ti:
                     e.scalar_tensor_tensor(dsl, pU[:, bank * 512:(bank + 1) * 512],
                                            sut[:, oc:oc + 1],
                                            SCR[:, ti * 512:(ti + 1) * 512],
                                            ALU.mult, ALU.mult), inc=s_v)
        p.wait("tensor", s_v, s_v.n)
        p.wait("tensor", s_s, s_s.n)
        p.wait("gpsimd", s_t, mmdone)

    if DEBUG_G:
        s_d = p.sem(name + "dbg")
        gd = mk.dram["gdbg"].rearrange("(c p) t -> c p t", p=128)
        p.wait("sync", s_v, s_v.n)
        for c in range(OC0):
            p.op("sync", lambda e, c=c:
                 e.dma_start(out=gd[c], in_=g_slice(mk, c, 0, NT)), inc=s_d, k=16)
        p.wait("sync", s_d, s_d.n)

    p.wait("tensor", s_v, s_v.n)
    emit_colsum(mk, lambda c, a, b: g_slice(mk, c, a, b), OC0, 2048,
                name + "csg", s_t, s_v, pG)

    last_store = {}
    for og in range(4):
        oc0 = og * 4
        for kcg in range(11):
            par = kcg % 2
            wtag = p.op("gpsimd", lambda e, og=og, kcg=kcg, par=par:
                        e.dma_start(out=WRG[:, par * 2048:(par + 1) * 2048],
                                    in_=wd[og, kcg]), inc=s_w, k=16)
            p.wait("tensor", s_w, wtag)
            for k4 in range(4):
                kc = kcg * 4 + k4
                for ti in range(2):
                    for j in range(4):
                        bank = j * 2 + ti
                        dst = pG if bank < 4 else pU
                        bo = (bank % 4) * 512
                        p.op("tensor",
                             lambda e, kc=kc, k4=k4, j=j, ti=ti, dst=dst, bo=bo, par=par:
                             e.matmul(dst[:, bo:bo + 512],
                                      WRG[:, par * 2048 + k4 * 512 + j * 128:
                                          par * 2048 + k4 * 512 + (j + 1) * 128],
                                      g_slice(mk, kc, ti * 512, (ti + 1) * 512),
                                      start=(kc == 0), stop=False), inc=s_t)
            if kcg >= 1:
                p.wait("gpsimd", s_t, s_t.n - 32)
        for ti in range(2):
            for j in range(4):
                bank = j * 2 + ti
                dst = pG if bank < 4 else pU
                bo = (bank % 4) * 512
                emit_zp_fix(mk, dst, bo, 2048, ti, s_t, True)
        mmdone = s_t.n
        p.wait("vector", s_t, mmdone)
        for j in range(4):
            oc = oc0 + j
            for ti in range(2):
                bank = j * 2 + ti
                src = pG if bank < 4 else pU
                bo = (bank % 4) * 512
                par = (oc * 2 + ti) % 4
                if par in last_store:
                    p.wait("sync", s_x, last_store[par])
                xt = p.op("sync", lambda e, oc=oc, ti=ti, par=par:
                          e.dma_start(out=XT[:, par * 512:(par + 1) * 512],
                                      in_=xin_t[oc, :, ti * 512:(ti + 1) * 512]),
                          inc=s_x, k=16)
                p.wait("vector", s_x, xt)
                tv = p.op("vector", lambda e, src=src, bo=bo, oc=oc, par=par:
                          e.scalar_tensor_tensor(XT[:, par * 512:(par + 1) * 512],
                                                 src[:, bo:bo + 512],
                                                 sdt[:, oc:oc + 1],
                                                 XT[:, par * 512:(par + 1) * 512],
                                                 ALU.mult, ALU.add), inc=s_v)
                p.wait("sync", s_v, tv)
                last_store[par] = p.op("sync", lambda e, oc=oc, ti=ti, par=par:
                                       e.dma_start(out=xout_t[oc, :, ti * 512:(ti + 1) * 512],
                                                   in_=XT[:, par * 512:(par + 1) * 512]),
                                       inc=s_x, k=16)
        p.wait("tensor", s_v, s_v.n)
        p.wait("gpsimd", s_t, mmdone)
    p.wait("sync", s_x, s_x.n)


def g_slice(mk, c, a, b, ntok=NT):
    if c < 22:
        return mk.sb["GA"][:, c * ntok + a: c * ntok + b]
    return mk.sb["GB"][:, (c - 22) * ntok + a: (c - 22) * ntok + b]



def emit_ffn1(mk, xin_t, xout_t, name, do_quant=True):
    p, sb = mk.p, mk.sb
    H, SCR, SCR4, SCRI, XT = sb["HB"], sb["SCR"], sb["SCR4"], sb["SCRI"], sb["XT"]
    RS, WRF, WRG = sb["RS"], sb["WRF"], sb["WRG"]
    pG, pU = mk.ps["PL"], mk.ps["PM"]
    s_w = p.sem(name + "w")
    s_t = p.esem("tensor")
    s_v = p.esem("vector")
    s_s = p.esem("scalar")
    s_g = p.esem("gpsimd")
    s_x = p.sem(name + "x")
    s_c = p.sem(name + "c")

    def h1_slice(c, a, b):
        if c < 22:
            return sb["GA"][:, c * NT + a: c * NT + b]
        if c < 44:
            return sb["GB"][:, (c - 22) * NT + a: (c - 22) * NT + b]
        return sb["BFC"][:, (c - 44) * NT + a: (c - 44) * NT + b]

    R_SS = RS[0:1, 2048:2050]
    R_MM = RS[0:1, 2052:2054]
    R_SQM = RS[0:1, 2054:2056]
    R_THR = RS[0:1, 2058:2060]
    RMSC = RS[:, 2060:2062]
    THRC = RS[:, 2062:2064]
    NTHRC = RS[:, 2064:2066]
    TMX = RS[:, 2066:2074]
    TMX2 = RS[:, 2074:2082]
    SROW = RS[0:1, 2112:3136]
    TRB = RS[0:8, 1024:1152]
    RECROW = RS[0:1, 0:1024]

    def wslab(wq, i):
        return wq[i] if wq.ndim == 3 else wq[i // 4, i % 4]

    if do_quant:
        w1, w2 = mk.dram["w1s"], mk.dram["w2s"]
        w1q, w2q = mk.dram["w1q"], mk.dram["w2q"]

        def wsq(w_dram, nslab, acc_off):
            first = True
            for i in range(nslab):
                for hh in range(4):
                    wt = p.op("sync", lambda e, i=i, hh=hh, w_dram=w_dram:
                              e.dma_start(out=WRF[:],
                                          in_=wslab(w_dram, i)[:, hh * 1024:(hh + 1) * 1024]),
                              inc=s_w, k=16)
                    p.wait("scalar", s_w, wt)
                    ts_ = p.op("scalar", lambda e:
                               e.activation(SCR[:, 0:1024], WRF[:], AF.Square),
                               inc=s_s)
                    p.wait("tensor", s_s, ts_)
                    for seg in range(2):
                        st_ = first and seg == 0
                        sp_ = (i == nslab - 1 and hh == 3 and seg == 1)
                        p.op("tensor",
                             lambda e, seg=seg, acc_off=acc_off, st_=st_, sp_=sp_:
                             e.matmul(pU[0:1, acc_off:acc_off + 512],
                                      sb["ONESB"][:, 0:1],
                                      SCR[:, seg * 512:(seg + 1) * 512],
                                      start=st_, stop=sp_), inc=s_t)
                    first = False
                    p.wait("scalar", s_t, s_t.n)
                    p.wait("sync", s_s, s_s.n)

        wsq(w1, 32, 0)
        wsq(w2, 32, 512)
        p.wait("vector", s_t, s_t.n)
        p.op("vector", lambda e:
             e.tensor_reduce(R_SS[:, 0:1], pU[0:1, 0:512], AX.X, ALU.add), inc=s_v)
        p.op("vector", lambda e:
             e.tensor_reduce(R_SS[:, 1:2], pU[0:1, 512:1024], AX.X, ALU.add), inc=s_v)
        tv = p.op("vector", lambda e:
                  e.tensor_scalar(R_MM, R_SS, 1.0 / (D * HID1), EPS_LN,
                                  ALU.mult, ALU.max), inc=s_v)
        p.wait("scalar", s_v, tv)
        ts_ = p.op("scalar", lambda e: e.activation(R_SQM, R_MM, AF.Sqrt), inc=s_s)
        p.wait("vector", s_s, ts_)
    else:
        s_r = p.sem(name + "r")
        rt = p.op("sync", lambda e: e.dma_start(out=RS[0:1, 2054:2056],
                                                in_=mk.dram["rmsv"]), inc=s_r, k=16)
        p.wait("vector", s_r, rt)
        w1q, w2q = mk.dram["w1q"], mk.dram["w2q"]

    tv = p.op("vector", lambda e:
              e.tensor_scalar(R_THR, R_SQM, 1.49, None, ALU.mult), inc=s_v)
    p.wait("tensor", s_v, tv)
    p.op("tensor", lambda e:
         e.matmul(pU[:, 1024:1026], sb["ONES1"][0:1, :], R_SQM,
                  start=True, stop=True), inc=s_t)
    mt = p.op("tensor", lambda e:
              e.matmul(pU[:, 1026:1028], sb["ONES1"][0:1, :], R_THR,
                       start=True, stop=True), inc=s_t)
    p.wait("vector", s_t, mt)
    p.op("vector", lambda e: e.tensor_copy(TMX[:, 0:2], pU[:, 1024:1026]), inc=s_v)
    p.op("vector", lambda e: e.reciprocal(RMSC, TMX[:, 0:2]), inc=s_v)
    p.op("vector", lambda e: e.tensor_copy(THRC, pU[:, 1026:1028]), inc=s_v)
    tv = p.op("vector", lambda e:
              e.tensor_scalar(NTHRC, pU[:, 1026:1028], -1.0, None, ALU.mult),
              inc=s_v)
    for e in ("tensor", "gpsimd", "scalar", "sync"):
        p.wait(e, s_v, tv)

    if do_quant:
        war_sc = {}

        def wquant(w_dram, wq_dram, nslab, wid):
            for i in range(nslab):
                for hh in range(4):
                    p.wait("sync", s_v, s_v.n)
                    wt = p.op("sync", lambda e, i=i, hh=hh, w_dram=w_dram:
                              e.dma_start(out=WRF[:],
                                          in_=wslab(w_dram, i)[:, hh * 1024:(hh + 1) * 1024]),
                              inc=s_w, k=16)
                    p.wait("vector", s_w, wt)
                    p.op("vector", lambda e, wid=wid:
                         e.tensor_scalar(WRF[:], WRF[:], RMSC[:, wid:wid + 1], 1.49,
                                         ALU.mult, ALU.min), inc=s_v)
                    sl = hh % 2
                    if sl in war_sc:
                        p.wait("vector", s_c, war_sc[sl])
                    tv = p.op("vector", lambda e, sl=sl:
                              e.tensor_scalar(SCRI[:, sl * 1024:(sl + 1) * 1024],
                                              WRF[:], -1.49, None, ALU.max), inc=s_v)
                    p.wait("gpsimd", s_v, tv)
                    war_sc[sl] = p.op("gpsimd",
                              lambda e, i=i, hh=hh, sl=sl, wq_dram=wq_dram:
                              e.dma_start(out=wslab(wq_dram, i)[:, hh * 1024:(hh + 1) * 1024],
                                          in_=SCRI[:, sl * 1024:(sl + 1) * 1024]),
                              inc=s_c, k=16)

        wquant(w1, w1q, 32, 0)
        wquant(w2, w2q, 32, 1)
        p.wait("vector", s_c, s_c.n)
        p.wait("gpsimd", s_c, s_c.n)
        if "rmsv" in mk.dram:
            rt = p.op("sync", lambda e: e.dma_start(out=mk.dram["rmsv"],
                                                    in_=RS[0:1, 2054:2056]),
                      inc=s_c, k=16)
            p.wait("sync", s_c, rt)

    def quant_act(src_slice_fn, nchunks, qname):
        for c in range(nchunks):
            if c == 0:
                p.op("vector", lambda e: e.tensor_copy(
                    RS[:, 0:1024], src_slice_fn(0, 0, NT)), inc=s_v)
            else:
                p.op("vector", lambda e, c=c: e.tensor_tensor(
                    RS[:, 0:1024], RS[:, 0:1024], src_slice_fn(c, 0, NT),
                    ALU.max), inc=s_v)
            tg = p.op("gpsimd", lambda e, c=c: e.tensor_scalar(
                SCR4[:, 0:1024], src_slice_fn(c, 0, NT), -1.0, None,
                ALU.mult), inc=s_g)
            p.wait("vector", s_g, tg)
            tv = p.op("vector", lambda e, c=c: e.tensor_tensor(
                RS[:, 0:1024], RS[:, 0:1024], SCR4[:, 0:1024],
                ALU.max), inc=s_v)
            p.wait("gpsimd", s_v, tv)
        p.wait("tensor", s_v, s_v.n)
        for seg in range(8):
            p.op("tensor", lambda e, seg=seg:
                 e.transpose(pU[:, seg * 128:(seg + 1) * 128],
                             RS[:, seg * 128:(seg + 1) * 128], sb["IDF"][:]),
                 inc=s_t)
        p.wait("vector", s_t, s_t.n)
        for seg in range(8):
            p.op("vector", lambda e, seg=seg:
                 e.tensor_reduce(TMX[:, seg:seg + 1],
                                 pU[:, seg * 128:(seg + 1) * 128], AX.X, ALU.max),
                 inc=s_v)
        p.op("vector", lambda e:
             e.tensor_scalar(TMX, TMX, EPS_LN, None, ALU.max), inc=s_v)
        p.op("vector", lambda e: e.reciprocal(TMX2, TMX), inc=s_v)
        tv = p.op("vector", lambda e:
                  e.tensor_scalar(TMX, TMX2, 127.0, None, ALU.mult), inc=s_v)
        p.wait("tensor", s_v, tv)
        mt = p.op("tensor", lambda e:
                  e.transpose(pU[0:8, 1024:1152], TMX, sb["IDF"][:]), inc=s_t)
        p.wait("vector", s_t, mt)
        tv = p.op("vector", lambda e:
                  e.tensor_copy(TRB, pU[0:8, 1024:1152]), inc=s_v)
        p.wait("sync", s_v, tv)
        sq = p.sem(qname)
        dtag = p.op("sync", lambda e:
                    e.dma_start(out=SROW, in_=TRB), inc=sq, k=16)
        p.wait("tensor", sq, dtag)
        for ti in range(2):
            p.op("tensor", lambda e, ti=ti:
                 e.matmul(pU[:, 1536:2048], sb["ONES1"][0:1, :],
                          SROW[:, ti * 512:(ti + 1) * 512], start=True, stop=True),
                 inc=s_t)
            mt = s_t.n
            p.wait("vector", s_t, mt)
            tv = p.op("vector", lambda e, ti=ti:
                      e.tensor_copy(RS[:, ti * 512:(ti + 1) * 512], pU[:, 1536:2048]),
                      inc=s_v)
            p.wait("tensor", s_v, tv)
        p.wait("gpsimd", s_v, s_v.n)
        for c in range(nchunks):
            for ti in range(2):
                tv = p.op("vector", lambda e, c=c, ti=ti:
                          e.tensor_tensor(SCRI[:, 0:512] if ti == 0 else SCRI[:, 512:1024],
                                          src_slice_fn(c, ti * 512, (ti + 1) * 512),
                                          RS[:, ti * 512:(ti + 1) * 512], ALU.mult),
                          inc=s_v)
                p.wait("gpsimd", s_v, tv)
                tg = p.op("gpsimd", lambda e, c=c, ti=ti:
                          e.tensor_copy(src_slice_fn(c, ti * 512, (ti + 1) * 512),
                                        SCRI[:, 0:512] if ti == 0 else SCRI[:, 512:1024]),
                          inc=s_g)
                p.wait("vector", s_g, tg)
        p.op("vector", lambda e: e.reciprocal(RECROW, SROW), inc=s_v)

    def build_br(rms_idx, extra_mult):
        p.op("vector", lambda e, rms_idx=rms_idx:
             e.tensor_scalar(RECROW, RECROW, R_SQM[:, rms_idx:rms_idx + 1], None,
                             ALU.mult), inc=s_v)
        tv = p.op("vector", lambda e, extra_mult=extra_mult:
                  e.tensor_scalar(RECROW, RECROW, extra_mult, None, ALU.mult),
                  inc=s_v)
        p.wait("tensor", s_v, tv)
        for ti in range(2):
            p.op("tensor", lambda e, ti=ti:
                 e.matmul(pU[:, 1536:2048], sb["ONES1"][0:1, :],
                          RECROW[:, ti * 512:(ti + 1) * 512], start=True, stop=True),
                 inc=s_t)
            mt = s_t.n
            p.wait("vector", s_t, mt)
            tv = p.op("vector", lambda e, ti=ti:
                      e.tensor_copy(RS[:, 1024 + ti * 512:1024 + (ti + 1) * 512],
                                    pU[:, 1536:2048]), inc=s_v)
            p.wait("tensor", s_v, tv)

    p.join(name + "jq", Prog.ENGS)
    quant_act(lambda c, a, b: H[:, c * NT + a: c * NT + b], KC, name + "qa")
    build_br(0, 1.0)
    p.wait("tensor", s_g, s_g.n)
    if do_quant:
        p.wait("gpsimd", s_c, s_c.n)

    mm_after = {}
    for og in range(32):
        par = og % 2
        if par in mm_after:
            p.wait("gpsimd", s_t, mm_after[par])
        wt = p.op("gpsimd", lambda e, og=og, par=par:
                  e.dma_start(out=WRG[:, par * 4096:(par + 1) * 4096],
                              in_=w1q[og]), inc=s_w, k=16)
        p.wait("tensor", s_w, wt)
        for kc in range(KC):
            for j in range(2):
                for ti in range(2):
                    bank = j * 2 + ti
                    p.op("tensor", lambda e, kc=kc, ti=ti, j=j, bank=bank, par=par:
                         e.matmul(pG[:, bank * 512:(bank + 1) * 512],
                                  WRG[:, par * 4096 + kc * 256 + j * 128:
                                      par * 4096 + kc * 256 + (j + 1) * 128],
                                  H[:, kc * NT + ti * 512: kc * NT + (ti + 1) * 512],
                                  start=(kc == 0), stop=(kc == KC - 1)), inc=s_t)
        mm_after[par] = s_t.n
        p.wait("vector", s_t, s_t.n)
        p.wait("scalar", s_t, s_t.n)
        for j in range(2):
            oc = og * 2 + j
            for ti in range(2):
                bank = j * 2 + ti
                if ("g", ti) in mm_after:
                    p.wait("vector", s_s, mm_after[("g", ti)])
                tv = p.op("vector", lambda e, bank=bank, ti=ti:
                          e.tensor_tensor(SCR[:, ti * 512:(ti + 1) * 512],
                                          pG[:, bank * 512:(bank + 1) * 512],
                                          RS[:, 1024 + ti * 512:1024 + (ti + 1) * 512],
                                          ALU.mult), inc=s_v)
                p.wait("scalar", s_v, tv)
                mm_after[("g", ti)] = p.op("scalar", lambda e, oc=oc, ti=ti:
                     e.activation(h1_slice(oc, ti * 512, (ti + 1) * 512),
                                  SCR[:, ti * 512:(ti + 1) * 512], AF.Gelu),
                     inc=s_s)
        p.wait("tensor", s_s, s_s.n)
        p.wait("tensor", s_v, s_v.n)

    if STOP_AFTER_MM1:
        p.join(name + "jd", Prog.ENGS)
        s_d = p.sem(name + "dbg")
        qd = mk.dram["qdbg"].rearrange("(c p) t -> c p t", p=128)
        for c in range(KC):
            p.op("sync", lambda e, c=c:
                 e.dma_start(out=qd[c], in_=H[:, c * NT:(c + 1) * NT]), inc=s_d, k=16)
        hd = mk.dram["h1dbg"].rearrange("(c p) t -> c p t", p=128)
        for c in range(OC1):
            p.op("sync", lambda e, c=c:
                 e.dma_start(out=hd[c], in_=h1_slice(c, 0, NT)), inc=s_d, k=16)
        p.op("sync", lambda e: e.dma_start(out=mk.dram["sdbg"], in_=SROW),
             inc=s_d, k=16)
        p.op("sync", lambda e: e.dma_start(out=mk.dram["brdbg"], in_=RS[:, 1024:2048]),
             inc=s_d, k=16)
        p.wait("sync", s_d, s_d.n)
        return

    p.join(name + "j2", Prog.ENGS)
    quant_act(h1_slice, OC1, name + "qb")
    build_br(1, 0.5)
    p.wait("tensor", s_g, s_g.n)

    last_store = {}
    mm_after = {}
    for og in range(8):
        for kcg in range(4):
            par = kcg % 2
            if par in mm_after:
                p.wait("gpsimd", s_t, mm_after[par])
            wt = p.op("gpsimd", lambda e, og=og, kcg=kcg, par=par:
                      e.dma_start(out=WRG[:, par * 4096:(par + 1) * 4096],
                                  in_=w2q[og, kcg]), inc=s_w, k=16)
            p.wait("tensor", s_w, wt)
            for k16 in range(KC):
                kc = kcg * KC + k16
                for ti in range(2):
                    for j in range(2):
                        bank = j * 2 + ti
                        p.op("tensor",
                             lambda e, kc=kc, k16=k16, ti=ti, j=j, bank=bank, par=par:
                             e.matmul(pG[:, bank * 512:(bank + 1) * 512],
                                      WRG[:, par * 4096 + k16 * 256 + j * 128:
                                          par * 4096 + k16 * 256 + (j + 1) * 128],
                                      h1_slice(kc, ti * 512, (ti + 1) * 512),
                                      start=(kc == 0), stop=(kc == OC1 - 1)),
                             inc=s_t)
            mm_after[par] = s_t.n
        p.wait("vector", s_t, s_t.n)
        for j in range(2):
            oc = og * 2 + j
            for ti in range(2):
                bank = j * 2 + ti
                par = (oc * 2 + ti) % 2
                if par in last_store:
                    p.wait("sync", s_x, last_store[par])
                xt = p.op("sync", lambda e, oc=oc, ti=ti, par=par:
                          e.dma_start(out=XT[:, par * 512:(par + 1) * 512],
                                      in_=xin_t[oc, :, ti * 512:(ti + 1) * 512]),
                          inc=s_x, k=16)
                p.wait("vector", s_x, xt)
                p.op("vector", lambda e, bank=bank, ti=ti:
                     e.tensor_tensor(SCR[:, ti * 512:(ti + 1) * 512],
                                     pG[:, bank * 512:(bank + 1) * 512],
                                     RS[:, 1024 + ti * 512:1024 + (ti + 1) * 512],
                                     ALU.mult), inc=s_v)
                tv = p.op("vector", lambda e, ti=ti, par=par:
                          e.tensor_tensor(XT[:, par * 512:(par + 1) * 512],
                                          XT[:, par * 512:(par + 1) * 512],
                                          SCR[:, ti * 512:(ti + 1) * 512], ALU.add),
                          inc=s_v)
                p.wait("sync", s_v, tv)
                last_store[par] = p.op("sync",
                                       lambda e, oc=oc, ti=ti, par=par:
                                       e.dma_start(out=xout_t[oc, :, ti * 512:(ti + 1) * 512],
                                                   in_=XT[:, par * 512:(par + 1) * 512]),
                                       inc=s_x, k=16)
        p.wait("tensor", s_v, s_v.n)
    p.wait("sync", s_x, s_x.n)


DEBUG_F1 = False
DEBUG_QA = False
STOP_AFTER_MM1 = False


def alloc_ffn1_bufs(mk):
    mk.sbuf("XT", [128, 1024], F32)
    mk.sbuf("RS", [128, 3136], F32)
    mk.sbuf("SCR", [128, 1024], BF16)
    mk.sbuf("SCR4", [128, 1024], BF16)
    mk.sbuf("SCRI", [128, 2048], I16)
    mk.sbuf("WRF", [128, 1024], F32)
    mk.sbuf("HB", [128, KC * NT], BF16)
    mk.sbuf("GA", [128, 22 * NT], BF16)
    mk.sbuf("GB", [128, 22 * NT], BF16)
    mk.sbuf("BFC", [128, 20 * NT], BF16)
    mk.sbuf("WRG", [128, 8192], BF16)
    mk.sbuf("IDF", [128, 128], F32)
    mk.psum("PL", [128, 2048], F32)
    mk.psum("PM", [128, 2048], F32)


def emit_identity(mk):
    p = mk.p
    g0 = p.sem("idini")
    p.op("gpsimd", lambda e: e.iota(mk.sb["RS"][:, 0:128], [[1, 128]], base=0,
                                    channel_multiplier=0,
                                    allow_small_or_imprecise_dtypes=True), inc=g0)
    p.op("gpsimd", lambda e: e.iota(mk.sb["RS"][:, 128:256], [[0, 128]], base=0,
                                    channel_multiplier=1,
                                    allow_small_or_imprecise_dtypes=True), inc=g0)
    p.wait("vector", g0, g0.n)
    s_v0 = p.esem("vector")
    tv = p.op("vector", lambda e: e.tensor_tensor(mk.sb["IDF"][:],
                                                  mk.sb["RS"][:, 0:128],
                                                  mk.sb["RS"][:, 128:256],
                                                  ALU.is_equal), inc=s_v0)
    for e in Prog.ENGS:
        p.wait(e, s_v0, tv)


def build_ffn1_launch(do_quant=True):
    mk = MK()
    xin = mk.dram_in("xin", [D, NT], F32)
    xout = mk.dram_out("xout", [D, NT], F32)
    if do_quant:
        mk.dram_in("w1s", [32, 128, KC * 256], F32)
        mk.dram_in("w2s", [8, 4, 128, KC * 256], F32)
        mk.dram_out("w1q", [32, 128, KC * 256], mybir.dt.int8)
        mk.dram_out("w2q", [8, 4, 128, KC * 256], mybir.dt.int8)
        mk.dram_out("rmsv", [1, 2], F32)
    else:
        mk.dram_in("w1q", [32, 128, KC * 256], mybir.dt.int8)
        mk.dram_in("w2q", [8, 4, 128, KC * 256], mybir.dt.int8)
        mk.dram_in("rmsv", [1, 2], F32)
    if DEBUG_F1 or STOP_AFTER_MM1:
        mk.dram_out("qdbg", [D, NT], BF16)
        mk.dram_out("h1dbg", [HID1, NT], BF16)
    if STOP_AFTER_MM1:
        mk.dram_out("sdbg", [1, 1024], F32)
        mk.dram_out("brdbg", [128, 1024], F32)

    alloc_consts(mk)
    alloc_ffn1_bufs(mk)
    emit_identity(mk)
    p = mk.p
    xin_t = xin.rearrange("(c p) t -> c p t", p=128)
    xout_t = xout.rearrange("(c p) t -> c p t", p=128)
    emit_ln(mk, xin_t, mk.sb["HB"], NT, "ln")
    p.join("pl", Prog.ENGS)
    emit_ffn1(mk, xin_t, xout_t, "f1", do_quant=do_quant)
    return mk.finish()


def stage_ffn1(inp):
    w1T = np.ascontiguousarray(inp["L1_ffn_w1"].T)
    w2T = np.ascontiguousarray(inp["L1_ffn_w2"].T)

    def slab1(wT):
        a = wT.reshape(KC, 128, 32, 256)
        return np.ascontiguousarray(a.transpose(2, 1, 0, 3).reshape(32, 128, KC * 256))

    def slab2(wT):
        a = wT.reshape(4, KC, 128, 8, 256)
        return np.ascontiguousarray(a.transpose(3, 0, 2, 1, 4).reshape(8, 4, 128, KC * 256))

    return {"w1s": slab1(w1T), "w2s": slab2(w2T)}


def run_ffn1(inp, x):
    nc = build_ffn1_launch(True)
    w = stage_ffn1(inp)
    in_maps = [dict(w, xin=s) for s in shard_x(x)]
    res = run_bass_kernel_spmd(nc, in_maps, core_ids=list(range(8)))
    return unshard_x([r["xout"] for r in res.results])



DEBUG_H = False
DEBUG_G = False


def build_ffn0_launch():
    mk = MK()
    xin = mk.dram_in("xin", [D, NT], F32)
    xout = mk.dram_out("xout", [D, NT], F32)
    if DEBUG_H:
        mk.dram_out("hdbg", [D, NT], BF16)
    if DEBUG_G:
        mk.dram_out("gdbg", [HID0, NT], BF16)
    mk.dram_in("wg0", [OC0 // 2, 128, KC * 256], I32)
    mk.dram_in("wu0", [OC0 // 2, 128, KC * 256], I32)
    mk.dram_in("wd0", [4, 11, 128, 2048], I32)
    mk.dram_in("sg0", [128, OC0], F32)
    mk.dram_in("su0", [128, OC0], F32)
    mk.dram_in("sd0", [128, KC], F32)

    alloc_consts(mk)
    mk.sbuf("XT", [128, 2 * NT], F32)
    mk.sbuf("RS", [128, 4096], F32)
    mk.sbuf("SCR", [128, 2048], BF16)
    mk.sbuf("HB", [128, KC * NT], BF16)
    mk.sbuf("GA", [128, 22 * NT], BF16)
    mk.sbuf("GB", [128, 22 * NT], BF16)
    mk.sbuf("WRG", [128, KC * 256], BF16)
    mk.sbuf("WRU", [128, KC * 256], BF16)
    mk.sbuf("SGB", [1, 4096], BF16)
    mk.sbuf("SG0", [128, OC0], F32)
    mk.sbuf("SU0", [128, OC0], F32)
    mk.sbuf("SD0", [128, KC], F32)
    mk.psum("PL", [128, 2048], F32)
    mk.psum("PM", [128, 2048], F32)

    p = mk.p
    s_in = p.sem("si")
    for nm in ("sg0", "su0", "sd0"):
        tag = p.op("sync", lambda e, nm=nm:
                   e.dma_start(out=mk.sb[nm.upper()][:], in_=mk.dram[nm]),
                   inc=s_in, k=16)
    for e in Prog.ENGS:
        p.wait(e, s_in, s_in.n)

    xin_t = xin.rearrange("(c p) t -> c p t", p=128)
    xout_t = xout.rearrange("(c p) t -> c p t", p=128)
    emit_ln(mk, xin_t, mk.sb["HB"], NT, "ln")
    p.join("pl", Prog.ENGS)
    if DEBUG_H:
        s_d = p.sem("dbg")
        hd = mk.dram["hdbg"].rearrange("(c p) t -> c p t", p=128)
        for c in range(KC):
            p.op("sync", lambda e, c=c:
                 e.dma_start(out=hd[c], in_=mk.sb["HB"][:, c * NT:(c + 1) * NT]),
                 inc=s_d, k=16)
        p.wait("sync", s_d, s_d.n)
    emit_ffn0(mk, xin_t, xout_t, "f0")
    return mk.finish()



def stage_ffn0(inp):
    wgT = np.ascontiguousarray(inp["L0_ffn_wq_gate"].T)
    wuT = np.ascontiguousarray(inp["L0_ffn_wq_up"].T)
    wdT = np.ascontiguousarray(inp["L0_ffn_wq_down"].T)

    def slab_gu(wT):
        a = wT.reshape(KC, 128, OC0 // 2, 256)
        return np.ascontiguousarray(a.transpose(2, 1, 0, 3).reshape(OC0 // 2, 128, KC * 256))

    def slab_d(wT):
        a = wT.reshape(11, 4, 128, 4, 512)
        return np.ascontiguousarray(a.transpose(3, 0, 2, 1, 4).reshape(4, 11, 128, 2048))

    sg = np.ascontiguousarray(inp["L0_ffn_s_gate"].astype(np.float32).reshape(OC0, 128).T)
    su = np.ascontiguousarray(inp["L0_ffn_s_up"].astype(np.float32).reshape(OC0, 128).T)
    sd = np.ascontiguousarray((inp["L0_ffn_s_down"].astype(np.float32) * 0.5).reshape(KC, 128).T)
    return {"wg0": slab_gu(wgT), "wu0": slab_gu(wuT), "wd0": slab_d(wdT),
            "sg0": sg, "su0": su, "sd0": sd}


def shard_x(x):
    shards = []
    for b in range(4):
        xt = np.ascontiguousarray(x[b].T)
        shards.append(np.ascontiguousarray(xt[:, 0:NT]))
        shards.append(np.ascontiguousarray(xt[:, NT:2 * NT]))
    return shards


def unshard_x(shards):
    out = np.empty((4, 2048, D), np.float32)
    for b in range(4):
        out[b, 0:NT] = shards[2 * b].T
        out[b, NT:2 * NT] = shards[2 * b + 1].T
    return out


def run_ffn0(inp, x):
    nc = build_ffn0_launch()
    w = stage_ffn0(inp)
    in_maps = [dict(w, xin=s) for s in shard_x(x)]
    res = run_bass_kernel_spmd(nc, in_maps, core_ids=list(range(8)))
    return unshard_x([r["xout"] for r in res.results])



def build_ln_launch():
    mk = MK()
    xin = mk.dram_in("xin", [D, NT], F32)
    mk.dram_out("hout", [KC, 128, NT], BF16)
    alloc_consts(mk)
    mk.sbuf("XT", [128, 1024], F32)
    mk.sbuf("RS", [128, 3072], F32)
    mk.sbuf("SCR", [128, 1024], BF16)
    mk.sbuf("HB", [128, KC * NT], BF16)
    mk.psum("PL", [128, 2048], F32)
    p = mk.p
    xin_t = xin.rearrange("(c p) t -> c p t", p=128)
    emit_ln(mk, xin_t, mk.sb["HB"], NT, "ln")
    p.join("pl", Prog.ENGS)
    s_d = p.sem("ho")
    for c in range(KC):
        p.op("sync", lambda e, c=c: e.dma_start(
            out=mk.dram["hout"][c], in_=mk.sb["HB"][:, c * NT:(c + 1) * NT]),
            inc=s_d, k=16)
    p.wait("sync", s_d, s_d.n)
    return mk.finish()



PI = 3.141592653589793
S5S = 2048
HBT_OFF = 0
QP_OFF = 0
QF_OFF = 16384
OS_OFF = 16384
KF_OFF = 32768
VF_OFF = 49152
KP_OFF = 65536


def emit_tables5(mk):
    p, sb = mk.p, mk.sb
    PL = mk.ps["PL"]
    RS = sb["RS"]
    s_t = p.esem("tensor")
    s_v = p.esem("vector")
    s_s = p.esem("scalar")
    s_g = p.esem("gpsimd")
    tg = p.op("gpsimd", lambda e: e.iota(
        RS[0:1, 0:1024], [[1, 1024]], base=0,
        channel_multiplier=0, allow_small_or_imprecise_dtypes=True), inc=s_g)
    p.wait("tensor", s_g, tg)
    OACC = mk.sb["OACC"]
    for st in range(2):
        p.wait("tensor", s_v, s_v.n)
        for ti in range(2):
            mt = p.op("tensor", lambda e, ti=ti: e.matmul(
                PL[:, ti * 512:(ti + 1) * 512], sb["ONES1"][0:1, :],
                RS[0:1, ti * 512:(ti + 1) * 512], start=True, stop=True),
                inc=s_t)
        p.wait("vector", s_t, mt)
        p.op("vector", lambda e, st=st: e.tensor_scalar(
            OACC[:, 1024:2048], PL[:, 0:1024], float(st * 1024), None,
            ALU.add), inc=s_v)
        p.op("vector", lambda e: e.tensor_scalar(
            RS[:, 1024:2048], OACC[:, 1024:2048], sb["INV5"][:, 0:1],
            None, ALU.mult), inc=s_v)
        for half, off in ((0, 0.0), (1, PI / 2.0)):
            p.op("vector", lambda e, off=off: e.tensor_scalar(
                RS[:, 2048:3072], RS[:, 1024:2048], off, None, ALU.add),
                inc=s_v)
            p.op("vector", lambda e: e.tensor_scalar(
                sb["SCRI"][:, 0:1024], RS[:, 2048:3072], 1.0 / TWO_PI, None,
                ALU.mult), inc=s_v)
            p.op("vector", lambda e: e.tensor_copy(
                OACC[:, 1024:2048], sb["SCRI"][:, 0:1024]), inc=s_v)
            tv = p.op("vector", lambda e: e.scalar_tensor_tensor(
                OACC[:, 0:1024], OACC[:, 1024:2048], -TWO_PI,
                RS[:, 2048:3072], ALU.mult, ALU.add), inc=s_v)
            p.wait("scalar", s_v, tv)
            dstt = sb["RPTS"] if half == 0 else sb["RPTC"]
            tsn = p.op("scalar", lambda e, dstt=dstt, st=st: e.activation(
                dstt[:, st * 1024:(st + 1) * 1024], OACC[:, 0:1024],
                AF.Sin), inc=s_s)
            p.wait("vector", s_s, tsn)
    p.wait("tensor", p.esem("vector"), p.esem("vector").n)
    p.wait("gpsimd", p.esem("scalar"), p.esem("scalar").n)


def emit_rope_tile(mk, sl, tbl_off):
    p, sb = mk.p, mk.sb
    s_x = p.sem("rp")
    s_v = p.esem("vector")
    s_g = p.esem("gpsimd")
    d1 = p.op("sync", lambda e: e.dma_start(out=sb["XQ"][0:64, :],
                                            in_=sl[64:128, :]), inc=s_x, k=16)
    d2 = p.op("sync", lambda e: e.dma_start(out=sb["XQ"][64:128, :],
                                            in_=sl[0:64, :]), inc=s_x, k=16)
    t1 = p.op("vector", lambda e: e.tensor_tensor(
        sb["SCR"][:, 0:1024], sl, sb["RPTC"][:, tbl_off:tbl_off + 1024],
        ALU.mult), inc=s_v)
    p.wait("vector", s_x, d2)
    t2 = p.op("vector", lambda e: e.scalar_tensor_tensor(
        sb["SCR4"][:, 0:1024], sb["XQ"][:], sb["SGN"][:, 0:1],
        sb["RPTS"][:, tbl_off:tbl_off + 1024], ALU.mult, ALU.mult), inc=s_v)
    t3 = p.op("vector", lambda e: e.tensor_tensor(
        sl, sb["SCR"][:, 0:1024], sb["SCR4"][:, 0:1024], ALU.add), inc=s_v)
    p.wait("sync", s_v, t3)


def emit_feats_tile(mk, ssl, dsl):
    p, sb = mk.p, mk.sb
    PL, PM = mk.ps["PL"], mk.ps["PM"]
    RS = sb["RS"]
    s_t = p.esem("tensor")
    s_v = p.esem("vector")
    s_s = p.esem("scalar")
    tq = p.op("scalar", lambda e: e.activation(sb["SCR"][:, 0:1024], ssl,
                                               AF.Square), inc=s_s)
    p.wait("tensor", s_s, tq)
    p.wait("tensor", p.esem("vector"), p.esem("vector").n)
    for ti in range(2):
        p.op("tensor", lambda e, ti=ti: e.matmul(
            PL[0:1, ti * 512:(ti + 1) * 512], sb["ONESB"][:, 0:1],
            sb["SCR"][:, ti * 512:(ti + 1) * 512], start=True, stop=True),
            inc=s_t)
    for ti in range(2):
        p.op("tensor", lambda e, ti=ti: e.matmul(
            PM[:, ti * 512:(ti + 1) * 512], sb["RTD"][:, 0:128],
            ssl[:, ti * 512:(ti + 1) * 512], start=True, stop=True), inc=s_t)
    p.wait("scalar", s_t, s_t.n)
    ts_ = p.op("scalar", lambda e: e.activation(
        RS[0:1, 1024:2048], PL[0:1, 0:1024], AF.Sqrt,
        bias=sb["EPS1"][0:1, 0:1], scale=1.0), inc=s_s)
    p.wait("vector", s_s, ts_)
    tv = p.op("vector", lambda e: e.reciprocal(
        RS[0:1, 0:1024], RS[0:1, 1024:2048]), inc=s_v)
    p.wait("tensor", s_v, tv)
    for ti in range(2):
        mt3 = p.op("tensor", lambda e, ti=ti: e.matmul(
            PL[:, 1024 + ti * 512:1024 + (ti + 1) * 512], sb["ONES1"][0:1, :],
            RS[0:1, ti * 512:(ti + 1) * 512], start=True, stop=True),
            inc=s_t)
    p.wait("vector", s_t, mt3)
    p.op("vector", lambda e: e.tensor_copy(
        RS[:, 1024:2048], PL[:, 1024:2048]), inc=s_v)
    p.op("vector", lambda e: e.tensor_tensor(
        RS[:, 2048:3072], PM[:, 0:1024], RS[:, 1024:2048], ALU.mult),
        inc=s_v)
    p.op("vector", lambda e: e.tensor_scalar(
        RS[:, 0:1024], RS[:, 2048:3072], sb["SHIFT"][:, 0:1], None, ALU.add),
        inc=s_v)
    p.op("vector", lambda e: e.tensor_scalar(
        sb["SCRI"][:, 0:1024], RS[:, 0:1024], 1.0 / TWO_PI, None, ALU.mult),
        inc=s_v)
    p.op("vector", lambda e: e.tensor_copy(
        RS[:, 1024:2048], sb["SCRI"][:, 0:1024]), inc=s_v)
    tv = p.op("vector", lambda e: e.scalar_tensor_tensor(
        RS[:, 2048:3072], RS[:, 1024:2048], -TWO_PI, RS[:, 0:1024],
        ALU.mult, ALU.add), inc=s_v)
    p.wait("scalar", s_v, tv)
    tsin = p.op("scalar", lambda e: e.activation(
        dsl, RS[:, 2048:3072], AF.Sin), inc=s_s)
    p.wait("vector", s_s, tsin)
    p.wait("tensor", s_s, tsin)


def emit_attn5(mk):
    p, sb = mk.p, mk.sb
    PL, PM = mk.ps["PL"], mk.ps["PM"]
    POOL = sb["POOL"]
    RS = sb["RS"]
    OACC = sb["OACC"]
    s_t = p.esem("tensor")
    s_v = p.esem("vector")
    s_s = p.esem("scalar")
    s_g = p.esem("gpsimd")
    s_w = p.sem("a5w")
    s_h = p.sem("a5h")
    s_x = p.sem("a5x")
    s_scr = p.sem("a5s")

    for nm, dnm in (("BQ5", "bq5"), ("BK5", "bk5"), ("BV5", "bv5"),
                    ("BO5", "bo5"), ("INV5", "inv5")):
        p.op("sync", lambda e, nm=nm, dnm=dnm: e.dma_start(
            out=sb[nm][:], in_=mk.dram[dnm]), inc=s_w, k=16)
    g0 = p.sem("a5c")
    p.op("gpsimd", lambda e: e.dma_start(out=sb["RTD"][:],
                                         in_=mk.dram["rtd"]), inc=g0, k=16)
    tg = p.op("gpsimd", lambda e: e.iota(RS[:, 0:1], [[0, 1]], base=0,
                                         channel_multiplier=1,
                                         allow_small_or_imprecise_dtypes=True),
              inc=g0)
    p.wait("vector", g0, tg)
    p.op("vector", lambda e: e.tensor_scalar(RS[:, 1:2], RS[:, 0:1],
                                             63.5, None, ALU.is_gt),
         inc=s_v)
    p.op("vector", lambda e: e.tensor_scalar(sb["SGN"][:, 0:1], RS[:, 1:2],
                                             2.0, -1.0, ALU.mult, ALU.add),
         inc=s_v)
    tv0 = p.op("vector", lambda e: e.tensor_scalar(
        sb["SHIFT"][:, 0:1], RS[:, 1:2], PI / 2.0, None, ALU.mult), inc=s_v)
    for e in Prog.ENGS:
        p.wait(e, s_w, s_w.n)
        p.wait(e, g0, g0.n)
        p.wait(e, s_v, tv0)

    emit_tables5(mk)
    if DEBUG_A5:
        p.join("a5t", Prog.ENGS)
        s_d0 = p.sem("a5t0")
        p.op("sync", lambda e: e.dma_start(out=mk.dram["rptcd"],
                                           in_=sb["RPTC"][:]), inc=s_d0, k=16)
        p.op("sync", lambda e: e.dma_start(out=mk.dram["rptsd"],
                                           in_=sb["RPTS"][:]), inc=s_d0, k=16)
        p.op("sync", lambda e: e.dma_start(out=mk.dram["sgnd"],
                                           in_=sb["SGN"][:]), inc=s_d0, k=16)
        p.op("sync", lambda e: e.dma_start(out=mk.dram["shfd"],
                                           in_=sb["SHIFT"][:]), inc=s_d0, k=16)
        p.wait("sync", s_d0, s_d0.n)
        p.join("a5t2", Prog.ENGS)

    hf = mk.dram["hfull"]
    slot_mm = {}
    psum_war = {}
    for half in range(2):
        if "hbt" in slot_mm:
            p.wait("sync", s_t, slot_mm["hbt"])
        for kc in range(KC):
            ht = p.op("sync", lambda e, kc=kc, half=half: e.dma_start(
                out=POOL[:, HBT_OFF + kc * 1024: HBT_OFF + (kc + 1) * 1024],
                in_=hf[kc][:, half * 1024:(half + 1) * 1024]), inc=s_h, k=16)
        p.wait("tensor", s_h, ht)
        og = 0
        for doff, wnm, bnm, noc in ((QF_OFF, "wq5", "BQ5", 8),
                                    (KF_OFF, "wk5", "BK5", 8),
                                    (VF_OFF, "wv5", "BV5", 8)):
            for oc in range(noc):
                slot = og % 2
                pr = og % 4
                if slot in slot_mm:
                    p.wait("gpsimd", s_t, slot_mm[slot])
                wt = p.op("gpsimd", lambda e, wnm=wnm, oc=oc, slot=slot:
                          e.dma_start(out=sb["WRG"][:, slot * 2048:
                                                    (slot + 1) * 2048],
                                      in_=mk.dram[wnm][oc]), inc=s_w, k=16)
                p.wait("tensor", s_w, wt)
                if pr in psum_war:
                    p.wait("tensor", s_s, psum_war[pr])
                pp = PL if pr < 2 else PM
                po = (pr % 2) * 1024
                for ti in range(2):
                    for kc in range(KC):
                        mmt = p.op("tensor",
                                   lambda e, slot=slot, kc=kc, ti=ti, pp=pp,
                                   po=po: e.matmul(
                                       pp[:, po + ti * 512:po + ti * 512 + 512],
                                       sb["WRG"][:, slot * 2048 + kc * 128:
                                                 slot * 2048 + (kc + 1) * 128],
                                       POOL[:, HBT_OFF + kc * 1024 + ti * 512:
                                            HBT_OFF + kc * 1024 + ti * 512 + 512],
                                       start=(kc == 0), stop=(kc == KC - 1)),
                                   inc=s_t)
                slot_mm[slot] = mmt
                slot_mm["hbt"] = mmt
                p.wait("scalar", s_t, mmt)
                psum_war[pr] = p.op("scalar",
                                    lambda e, doff=doff, oc=oc, bnm=bnm,
                                    pp=pp, po=po, half=half: e.activation(
                                        POOL[:, doff + oc * 2048 + half * 1024:
                                             doff + oc * 2048 + half * 1024 + 1024],
                                        pp[:, po:po + 1024], AF.Identity,
                                        bias=sb[bnm][:, oc:oc + 1], scale=1.0),
                                    inc=s_s)
                og += 1
    p.join("a5p", Prog.ENGS)

    for doff in (QF_OFF, KF_OFF):
        for h in range(8):
            for st in range(2):
                emit_rope_tile(mk, POOL[:, doff + h * 2048 + st * 1024:
                                        doff + h * 2048 + st * 1024 + 1024],
                               st * 1024)
    p.join("a5r", Prog.ENGS)

    if DEBUG_A5:
        s_d = p.sem("a5d")
        for h in range(8):
            p.op("sync", lambda e, h=h: e.dma_start(
                out=mk.dram["qfd"][h],
                in_=POOL[:, QF_OFF + h * 2048:QF_OFF + (h + 1) * 2048]),
                inc=s_d, k=16)
            p.op("sync", lambda e, h=h: e.dma_start(
                out=mk.dram["kfd"][h],
                in_=POOL[:, KF_OFF + h * 2048:KF_OFF + (h + 1) * 2048]),
                inc=s_d, k=16)
            p.op("sync", lambda e, h=h: e.dma_start(
                out=mk.dram["vfd"][h],
                in_=POOL[:, VF_OFF + h * 2048:VF_OFF + (h + 1) * 2048]),
                inc=s_d, k=16)
        p.wait("sync", s_d, s_d.n)
        p.join("a5d1", Prog.ENGS)

    for soff, foff in ((QF_OFF, QP_OFF), (KF_OFF, KP_OFF)):
        for h in range(8):
            for st in range(2):
                sl = POOL[:, soff + h * 2048 + st * 1024:
                          soff + h * 2048 + st * 1024 + 1024]
                dl = POOL[:, foff + h * 2048 + st * 1024:
                          foff + h * 2048 + st * 1024 + 1024]
                emit_feats_tile(mk, sl, dl)
    p.join("a5f", Prog.ENGS)
    if DEBUG_A5:
        for h in range(8):
            p.op("sync", lambda e, h=h: e.dma_start(
                out=mk.dram["qpd"][h],
                in_=POOL[:, QP_OFF + h * 2048:QP_OFF + (h + 1) * 2048]),
                inc=s_d, k=16)
            p.op("sync", lambda e, h=h: e.dma_start(
                out=mk.dram["kpd"][h],
                in_=POOL[:, KP_OFF + h * 2048:KP_OFF + (h + 1) * 2048]),
                inc=s_d, k=16)
        p.wait("sync", s_d, s_d.n)
        p.join("a5d2", Prog.ENGS)

    gtag = {}
    for h in range(8):
        for st in range(2):
            for j in range(8):
                if "row" in gtag:
                    p.wait("tensor", s_v, gtag["row"])
                if "cs" in gtag:
                    p.wait("vector", s_t, gtag["cs"])
                tv = p.op("vector", lambda e, h=h, j=j, st=st: e.tensor_tensor(
                    sb["SCR"][:, 0:1024],
                    POOL[:, QP_OFF + h * 2048 + st * 1024:
                         QP_OFF + h * 2048 + st * 1024 + 1024],
                    POOL[:, KP_OFF + j * 2048 + st * 1024:
                         KP_OFF + j * 2048 + st * 1024 + 1024],
                    ALU.mult), inc=s_v)
                p.wait("tensor", s_v, tv)
                for ti in range(2):
                    mmt = p.op("tensor", lambda e, ti=ti: e.matmul(
                        PL[0:1, ti * 512:(ti + 1) * 512], sb["ONESB"][:, 0:1],
                        sb["SCR"][:, ti * 512:(ti + 1) * 512],
                        start=True, stop=True), inc=s_t)
                gtag["cs"] = mmt
                p.wait("vector", s_t, mmt)
                gtag["row"] = p.op("vector", lambda e: e.tensor_scalar(
                    RS[0:1, 2048:3072], PL[0:1, 0:1024], 2.0 / 128.0, None,
                    ALU.mult), inc=s_v)
                p.wait("tensor", s_v, gtag["row"])
                if "tmp" in gtag:
                    p.wait("tensor", s_v, gtag["tmp"])
                for ti in range(2):
                    mmt = p.op("tensor", lambda e, ti=ti: e.matmul(
                        PM[:, 1024 + ti * 512:1024 + (ti + 1) * 512],
                        sb["ONES1"][0:1, :],
                        RS[0:1, 2048 + ti * 512:2048 + (ti + 1) * 512],
                        start=True, stop=True), inc=s_t)
                p.wait("vector", s_t, mmt)
                if j == 0 and "sc" in gtag:
                    p.wait("vector", s_g, gtag["sc"])
                if j == 0:
                    gtag["tmp"] = p.op("vector", lambda e, j=j, st=st:
                                       e.tensor_tensor(
                                           OACC[:, st * 1024:(st + 1) * 1024],
                                           POOL[:, VF_OFF + j * 2048 + st * 1024:
                                                VF_OFF + j * 2048 + st * 1024 + 1024],
                                           PM[:, 1024:2048], ALU.mult), inc=s_v)
                else:
                    gtag["tmp"] = p.op("vector", lambda e, j=j, st=st:
                                       e.tensor_tensor(
                                           sb["SCR4"][:, 0:1024],
                                           POOL[:, VF_OFF + j * 2048 + st * 1024:
                                                VF_OFF + j * 2048 + st * 1024 + 1024],
                                           PM[:, 1024:2048], ALU.mult), inc=s_v)
                    gtag["acc"] = p.op("vector", lambda e, st=st: e.tensor_tensor(
                        OACC[:, st * 1024:(st + 1) * 1024],
                        OACC[:, st * 1024:(st + 1) * 1024],
                        sb["SCR4"][:, 0:1024], ALU.add), inc=s_v)
        oav = OACC[:].rearrange("p (a r) -> p r a", r=16)
        p.wait("gpsimd", s_v, gtag["acc"])
        for r in range(16):
            gtag["sc"] = p.op("gpsimd", lambda e, r=r, h=h, oav=oav:
                              e.tensor_copy(
                                  POOL[:, OS_OFF + r * 1024 + h * 128:
                                       OS_OFF + r * 1024 + h * 128 + 128],
                                  oav[:, r]), inc=s_g)
    p.join("a5g", Prog.ENGS)
    if DEBUG_A5:
        for r in range(16):
            p.op("sync", lambda e, r=r: e.dma_start(
                out=mk.dram["osd"][r],
                in_=POOL[:, OS_OFF + r * 1024:OS_OFF + (r + 1) * 1024]),
                inc=s_d, k=16)
        p.wait("sync", s_d, s_d.n)
        p.join("a5d3", Prog.ENGS)

    xin_t = mk.dram["xin"].rearrange("(c p) t -> c p t", p=128)
    xout_t = mk.dram["xout"].rearrange("(c p) t -> c p t", p=128)
    slot_mm = {}
    out_war = {}
    st_tag = {}
    for oc in range(KC):
        slot = oc % 2
        par = oc % 2
        if slot in slot_mm:
            p.wait("gpsimd", s_t, slot_mm[slot])
        wt = p.op("gpsimd", lambda e, oc=oc, slot=slot: e.dma_start(
            out=sb["WRG"][:, slot * 2048:(slot + 1) * 2048],
            in_=mk.dram["wo5"][oc]), inc=s_w, k=16)
        if "x" in st_tag:
            p.wait("sync", s_v, st_tag["x"])
        xt = p.op("sync", lambda e, oc=oc: e.dma_start(
            out=RS[:, 2048:3072], in_=xin_t[oc]), inc=s_x, k=16)
        p.wait("tensor", s_w, wt)
        if par in out_war:
            p.wait("tensor", s_v, out_war[par])
        pp = PL if par == 0 else PM
        for ti in range(2):
            for rc in range(KC):
                mmt = p.op("tensor", lambda e, slot=slot, rc=rc, ti=ti, pp=pp:
                           e.matmul(pp[:, ti * 512:(ti + 1) * 512],
                                    sb["WRG"][:, slot * 2048 + rc * 128:
                                              slot * 2048 + (rc + 1) * 128],
                                    POOL[:, OS_OFF + rc * 1024 + ti * 512:
                                         OS_OFF + rc * 1024 + ti * 512 + 512],
                                    start=(rc == 0), stop=(rc == KC - 1)),
                           inc=s_t)
        slot_mm[slot] = mmt
        p.wait("vector", s_t, mmt)
        p.wait("vector", s_x, xt)
        sl2 = oc % 2
        if ("o", sl2) in st_tag:
            p.wait("vector", s_x, st_tag[("o", sl2)])
        st_tag["x"] = p.op("vector", lambda e, oc=oc, pp=pp, sl2=sl2:
                           e.scalar_tensor_tensor(
                               RS[:, sl2 * 1024:(sl2 + 1) * 1024],
                               pp[:, 0:1024], sb["BO5"][:, oc:oc + 1],
                               RS[:, 2048:3072], ALU.add, ALU.add), inc=s_v)
        out_war[par] = st_tag["x"]
        p.wait("sync", s_v, st_tag["x"])
        st_tag[("o", sl2)] = p.op("sync", lambda e, oc=oc, sl2=sl2:
                                  e.dma_start(out=xout_t[oc],
                                              in_=RS[:, sl2 * 1024:
                                                     (sl2 + 1) * 1024]),
                                  inc=s_x, k=16)
    p.wait("sync", s_x, s_x.n)


DEBUG_A5 = False


def build_attn5_launch():
    mk = MK()
    mk.dram_in("xin", [D, NT], F32)
    mk.dram_out("xout", [D, NT], F32)
    if DEBUG_A5:
        for nm in ("qfd", "kfd", "vfd", "qpd", "kpd"):
            mk.dram_out(nm, [8, 128, 2048], BF16)
        mk.dram_out("osd", [16, 128, 1024], BF16)
        mk.dram_out("rptcd", [128, 2048], BF16)
        mk.dram_out("rptsd", [128, 2048], BF16)
        mk.dram_out("sgnd", [128, 1], F32)
        mk.dram_out("shfd", [128, 1], F32)
    mk.dram_in("hfull", [KC, 128, S5S], BF16)
    mk.dram_in("wq5", [8, 128, 2048], F32)
    mk.dram_in("wk5", [8, 128, 2048], F32)
    mk.dram_in("wv5", [8, 128, 2048], F32)
    mk.dram_in("wo5", [16, 128, 2048], F32)
    mk.dram_in("bq5", [128, 8], F32)
    mk.dram_in("bk5", [128, 8], F32)
    mk.dram_in("bv5", [128, 8], F32)
    mk.dram_in("bo5", [128, 16], F32)
    mk.dram_in("rtd", [128, 128], F32)
    mk.dram_in("inv5", [128, 1], F32)

    alloc_consts(mk)
    mk.sbuf("POOL", [128, 81920], BF16)
    mk.sbuf("OACC", [128, 2048], F32)
    mk.sbuf("RS", [128, 3072], F32)
    mk.sbuf("SCR", [128, 1024], BF16)
    mk.sbuf("SCR4", [128, 1024], BF16)
    mk.sbuf("SCRI", [128, 1024], I16)
    mk.sbuf("WRG", [128, 4096], BF16)
    mk.sbuf("RPTC", [128, 2048], BF16)
    mk.sbuf("RPTS", [128, 2048], BF16)
    mk.sbuf("XQ", [128, 1024], BF16)
    mk.sbuf("INV5", [128, 1], F32)
    mk.sbuf("RTD", [128, 128], BF16)
    mk.sbuf("SGN", [128, 1], F32)
    mk.sbuf("SHIFT", [128, 1], F32)
    mk.sbuf("BQ5", [128, 8], F32)
    mk.sbuf("BK5", [128, 8], F32)
    mk.sbuf("BV5", [128, 8], F32)
    mk.sbuf("BO5", [128, 16], F32)
    mk.psum("PL", [128, 2048], F32)
    mk.psum("PM", [128, 2048], F32)
    emit_attn5(mk)
    return mk.finish()


def stage_attn1(inp):
    def slabA(A, noc):
        a = np.ascontiguousarray(np.asarray(A, np.float32)).reshape(
            KC, 128, noc, 128)
        return np.ascontiguousarray(
            a.transpose(2, 1, 0, 3).reshape(noc, 128, 2048))

    def bcol(b, noc):
        return np.ascontiguousarray(np.asarray(b, np.float32).reshape(noc, 128).T)

    rt = np.asarray(inp["L1_attn_rand_matrix"], np.float32).T
    inv = (1.0 / 10000.0 ** (np.arange(0, 128, 2, dtype=np.float32) / 128.0))
    wqT = np.asarray(inp["L1_attn_Wq"], np.float32).T
    base = {
        "wk5": slabA(np.asarray(inp["L1_attn_Wk"]).T, 8),
        "wv5": slabA(np.asarray(inp["L1_attn_Wv"]).T, 8),
        "wo5": slabA(np.asarray(inp["L1_attn_Wo"]).T, 16),
        "bk5": bcol(inp["L1_attn_bk"], 8),
        "bv5": bcol(inp["L1_attn_bv"], 8),
        "bo5": bcol(inp["L1_attn_bo"], 16),
        "rtd": np.ascontiguousarray(np.concatenate([rt, rt], axis=1)),
        "inv5": np.ascontiguousarray(
            np.concatenate([inv, inv]).reshape(128, 1).astype(np.float32)),
    }
    per = []
    bq = np.asarray(inp["L1_attn_bq"], np.float32)
    for half in range(2):
        sl = slice(half * 1024, (half + 1) * 1024)
        per.append(dict(base,
                        wq5=slabA(wqT[:, sl], 8),
                        bq5=bcol(bq[sl], 8)))
    return per


def run_attn1(inp, x):
    ncl = _get("lnx", build_ln_launch)
    shards = shard_x(x)
    res = run_bass_kernel_spmd(ncl, [{"xin": s} for s in shards],
                               core_ids=list(range(8)))
    houts = [r["hout"] for r in res.results]
    wa = stage_attn1(inp)
    nc5 = _get("a5", build_attn5_launch)
    in_maps = []
    for c in range(8):
        pc = 2 * (c // 2)
        hfull = np.concatenate([houts[pc], houts[pc + 1]], axis=2)
        in_maps.append(dict(wa[c % 2], xin=shards[c],
                            hfull=np.ascontiguousarray(hfull)))
    res = run_bass_kernel_spmd(nc5, in_maps, core_ids=list(range(8)))
    return unshard_x([r["xout"] for r in res.results])



SCALE0 = 0.08838834764831845


def emit_attn2(mk):
    p, sb = mk.p, mk.sb
    PL, PM = mk.ps["PL"], mk.ps["PM"]
    RS = sb["RS"]
    s_t = p.esem("tensor")
    s_v = p.esem("vector")
    s_s = p.esem("scalar")
    s_g = p.esem("gpsimd")
    s_w = p.sem("a2w")
    s_h = p.sem("a2h")
    s_x = p.sem("a2x")

    def tsync():
        p.wait("tensor", s_v, s_v.n)
        p.wait("tensor", s_s, s_s.n)
        p.wait("tensor", s_g, s_g.n)

    def vsync():
        p.wait("vector", s_t, s_t.n)

    def ssync():
        p.wait("scalar", s_t, s_t.n)
        p.wait("scalar", s_v, s_v.n)

    p.op("sync", lambda e: e.dma_start(out=sb["INV5"][:],
                                       in_=mk.dram["inv2"]), inc=s_w, k=16)
    g0 = p.sem("a2c")
    p.op("gpsimd", lambda e: e.dma_start(out=sb["WB"][:],
                                         in_=mk.dram["wb2"]), inc=g0, k=16)
    p.op("gpsimd", lambda e: e.memset(sb["TWOS1"][:], 2.0), inc=g0)
    p.op("gpsimd", lambda e: e.memset(sb["EPS6"][:], 1e-6), inc=g0)
    p.op("gpsimd", lambda e: e.memset(sb["ONESR"][:], 1.0), inc=g0)
    tg = p.op("gpsimd", lambda e: e.iota(RS[:, 0:1], [[0, 1]], base=0,
                                         channel_multiplier=1,
                                         allow_small_or_imprecise_dtypes=True),
              inc=g0)
    p.wait("vector", g0, tg)
    p.op("vector", lambda e: e.tensor_scalar(RS[:, 1:2], RS[:, 0:1],
                                             63.5, None, ALU.is_gt), inc=s_v)
    p.op("vector", lambda e: e.tensor_scalar(sb["SGN"][:, 0:1], RS[:, 1:2],
                                             2.0, -1.0, ALU.mult, ALU.add),
         inc=s_v)
    tv0 = p.op("vector", lambda e: e.tensor_scalar(
        sb["SHIFT"][:, 0:1], RS[:, 1:2], PI / 2.0, None, ALU.mult), inc=s_v)
    for e in Prog.ENGS:
        p.wait(e, s_w, s_w.n)
        p.wait(e, g0, g0.n)
        p.wait(e, s_v, tv0)

    emit_tables5(mk)
    tvi = p.op("vector", lambda e: e.tensor_copy(sb["IOR"][:],
                                                 RS[0:1, 0:1024]), inc=s_v)
    p.wait("tensor", s_v, tvi)

    tsync()
    p.op("tensor", lambda e: e.matmul(PL[:, 0:512], sb["ONES1"][0:1, :],
                                      sb["IOR"][0:1, 0:512],
                                      start=True, stop=True), inc=s_t)
    p.op("tensor", lambda e: e.matmul(PL[:, 512:1024], sb["ONES1"][0:1, :],
                                      sb["IOR"][0:1, 512:1024],
                                      start=True, stop=True), inc=s_t)
    mt = p.op("tensor", lambda e: e.matmul(PM[:, 0:512], sb["IOR"][0:1, 0:128],
                                           sb["ONESR"][0:1, 0:512],
                                           start=True, stop=True), inc=s_t)
    p.wait("vector", s_t, mt)
    p.op("vector", lambda e: e.tensor_copy(sb["GC"][:], PL[:, 0:512]), inc=s_v)
    p.op("vector", lambda e: e.tensor_copy(sb["GR"][:], PM[:, 0:512]), inc=s_v)
    for r in range(4):
        p.op("vector", lambda e, r=r: e.tensor_scalar(
            RS[:, 1024:1536], sb["GR"][:], float(128 * r), None, ALU.add),
            inc=s_v)
        p.op("vector", lambda e, r=r: e.tensor_tensor(
            sb["MSK"][:, r * 512:(r + 1) * 512], sb["GC"][:],
            RS[:, 1024:1536], ALU.is_ge), inc=s_v)
    p.op("vector", lambda e: e.tensor_tensor(
        sb["SLM"][:], sb["GR"][:, 0:128], sb["GC"][:, 0:128], ALU.is_gt),
        inc=s_v)
    p.op("vector", lambda e: e.tensor_tensor(
        sb["SUM"][:], sb["GR"][:, 0:128], sb["GC"][:, 0:128], ALU.is_lt),
        inc=s_v)
    tvm = p.op("vector", lambda e: e.tensor_tensor(
        sb["IDB"][:], sb["GR"][:, 0:128], sb["GC"][:, 0:128], ALU.is_equal),
        inc=s_v)
    p.wait("tensor", s_v, tvm)

    hf = mk.dram["hfull"]
    for kc in range(KC):
        ht = p.op("sync", lambda e, kc=kc: e.dma_start(
            out=sb["HB"][:, kc * 2048:(kc + 1) * 2048], in_=hf[kc]),
            inc=s_h, k=16)
    for e in Prog.ENGS:
        p.wait(e, s_h, s_h.n)

    for ti in range(4):
        for kc in range(KC):
            mt = p.op("tensor", lambda e, ti=ti, kc=kc: e.matmul(
                PL[0:8, ti * 512:(ti + 1) * 512],
                sb["WB"][:, kc * 8:(kc + 1) * 8],
                sb["HB"][:, kc * 2048 + ti * 512:kc * 2048 + ti * 512 + 512],
                start=(kc == 0), stop=(kc == KC - 1)), inc=s_t)
    p.wait("scalar", s_t, mt)
    ts_ = p.op("scalar", lambda e: e.activation(
        sb["BETA8"][:], PL[0:8, 0:2048], AF.Sigmoid), inc=s_s)
    p.wait("tensor", s_s, ts_)

    war_w = {}
    for h in range(8):
        for dst, wnm in (("QF2", "wq2"), ("KF2", "wk2"), ("VF2", "wv2")):
            slot = {"QF2": 0, "KF2": 1, "VF2": 0}[dst] if False else \
                   ("QF2", "KF2", "VF2").index(dst) % 2
            if slot in war_w:
                p.wait("gpsimd", s_t, war_w[slot])
            wt = p.op("gpsimd", lambda e, wnm=wnm, h=h, slot=slot:
                      e.dma_start(out=sb["WRG"][:, slot * 2048:
                                                (slot + 1) * 2048],
                                  in_=mk.dram[wnm][h]), inc=s_w, k=16)
            p.wait("tensor", s_w, wt)
            tsync()
            for ti in range(4):
                for kc in range(KC):
                    mt = p.op("tensor", lambda e, ti=ti, kc=kc, slot=slot:
                              e.matmul(PL[:, ti * 512:(ti + 1) * 512],
                                       sb["WRG"][:, slot * 2048 + kc * 128:
                                                 slot * 2048 + (kc + 1) * 128],
                                       sb["HB"][:, kc * 2048 + ti * 512:
                                                kc * 2048 + ti * 512 + 512],
                                       start=(kc == 0), stop=(kc == KC - 1)),
                              inc=s_t)
            war_w[slot] = mt
            p.wait("scalar", s_t, mt)
            ts_ = p.op("scalar", lambda e, dst=dst: e.activation(
                sb[dst][:], PL[:, 0:2048], AF.Identity), inc=s_s)
            p.wait("tensor", s_s, ts_)
            p.wait("vector", s_s, ts_)

        ts_ = p.op("scalar", lambda e: e.activation(
            sb["SCR2"][:], sb["KF2"][:], AF.Square), inc=s_s)
        p.wait("tensor", s_s, ts_)
        for ti in range(4):
            mt = p.op("tensor", lambda e, ti=ti: e.matmul(
                PM[0:1, 1024 + ((ti % 2) * 512):1024 + (ti % 2) * 512 + 512]
                if False else
                PL[0:1, ti * 512:(ti + 1) * 512], sb["ONESB"][:, 0:1],
                sb["SCR2"][:, ti * 512:(ti + 1) * 512], start=True, stop=True),
                inc=s_t)
        p.wait("scalar", s_t, mt)
        ts_ = p.op("scalar", lambda e: e.activation(
            RS[0:1, 1024:3072], PL[0:1, 0:2048], AF.Sqrt,
            bias=sb["EPS6"][0:1, 0:1], scale=1.0), inc=s_s)
        p.wait("vector", s_s, ts_)
        tv = p.op("vector", lambda e: e.reciprocal(
            sb["OACC"][0:1, 0:2048], RS[0:1, 1024:3072]), inc=s_v)
        tsync()
        for ti in range(4):
            mt = p.op("tensor", lambda e, ti=ti: e.matmul(
                PL[:, ti * 512:(ti + 1) * 512], sb["ONES1"][0:1, :],
                sb["OACC"][0:1, ti * 512:ti * 512 + 512],
                start=True, stop=True), inc=s_t)
        p.wait("vector", s_t, mt)
        for ti in range(2):
            tv = p.op("vector", lambda e, ti=ti: e.tensor_tensor(
                sb["KF2"][:, ti * 1024:(ti + 1) * 1024],
                sb["KF2"][:, ti * 1024:(ti + 1) * 1024],
                PL[:, ti * 1024:(ti + 1) * 1024], ALU.mult), inc=s_v)
        p.wait("tensor", s_v, tv)

        for st in range(2):
            emit_rope_tile(mk, sb["QF2"][:, st * 1024:(st + 1) * 1024],
                           st * 1024)
            emit_rope_tile(mk, sb["KF2"][:, st * 1024:(st + 1) * 1024],
                           st * 1024)

        bt_ = p.op("sync", lambda e, h=h: e.dma_start(
            out=sb["BR1"][:], in_=sb["BETA8"][h:h + 1, :]), inc=s_x, k=16)
        p.wait("tensor", s_x, bt_)
        tsync()
        for ti in range(4):
            mt = p.op("tensor", lambda e, ti=ti: e.matmul(
                PL[:, ti * 512:(ti + 1) * 512], sb["TWOS1"][0:1, :],
                sb["BR1"][0:1, ti * 512:(ti + 1) * 512],
                start=True, stop=True), inc=s_t)
        p.wait("vector", s_t, mt)
        for ti in range(2):
            tv = p.op("vector", lambda e, ti=ti: e.tensor_tensor(
                sb["KB2"][:, ti * 1024:(ti + 1) * 1024],
                sb["KF2"][:, ti * 1024:(ti + 1) * 1024],
                PL[:, ti * 1024:(ti + 1) * 1024], ALU.mult), inc=s_v)
        p.wait("tensor", s_v, tv)

        tv = p.op("vector", lambda e: e.memset(sb["SF32"][:], 0.0), inc=s_v)
        tv = p.op("vector", lambda e: e.memset(sb["SBF"][:], 0.0), inc=s_v)
        p.wait("tensor", s_v, tv)
        PTI = sb["PTI"]
        for cc in range(KC):
            sl = slice(cc * 128, (cc + 1) * 128)

            def mm(dst_off, lhsT, rhs):
                tsync()
                return p.op("tensor", lambda e, dst_off=dst_off, lhsT=lhsT,
                            rhs=rhs: e.matmul(PM[:, dst_off:dst_off + 128],
                                              lhsT, rhs, start=True,
                                              stop=True), inc=s_t)

            PMT = mk.ps["PMT"]

            def tr(src):
                tsync()
                return p.op("tensor", lambda e, src=src:
                            e.transpose(PMT[:, 0:128], src,
                                        sb["IDB"][:]), inc=s_t)

            def dve(fn):
                vsync()
                t = p.op("vector", fn, inc=s_v)
                p.wait("tensor", s_v, t)
                return t

            kf, kb, vf = sb["KF2"][:, sl], sb["KB2"][:, sl], sb["VF2"][:, sl]
            mm(0, kf, kb)
            mm(128, kb, kf)
            dve(lambda e: e.scalar_tensor_tensor(
                PTI[:, 0:128], PM[:, 0:128], -1.0, sb["SLM"][:],
                ALU.mult, ALU.mult))
            dve(lambda e: e.scalar_tensor_tensor(
                PTI[:, 128:256], PM[:, 128:256], -1.0, sb["SUM"][:],
                ALU.mult, ALU.mult))
            mm(256, sb["SBF"][:], kf)
            dve(lambda e, vf=vf: e.tensor_tensor(
                sb["RHT"][:], vf, PM[:, 256:384], ALU.subtract))
            tr(sb["RHT"][:])
            dve(lambda e: e.tensor_copy(sb["RB"][:], PMT[:, 0:128]))
            pcur, pnxt = 0, 256
            for j in range(7):
                mm(512, PTI[:, pcur + 128:pcur + 256], sb["RB"][:])
                if j == 6:
                    dve(lambda e, sl=sl: e.tensor_tensor(
                        sb["VN"][:, sl], sb["RB"][:], PM[:, 512:640], ALU.add))
                else:
                    dve(lambda e: e.tensor_tensor(
                        sb["RB"][:], sb["RB"][:], PM[:, 512:640], ALU.add))
                    mm(640, PTI[:, pcur + 128:pcur + 256], PTI[:, pcur:pcur + 128])
                    mm(768, PTI[:, pcur:pcur + 128], PTI[:, pcur + 128:pcur + 256])
                    dve(lambda e, pnxt=pnxt: e.tensor_copy(
                        PTI[:, pnxt:pnxt + 128], PM[:, 640:768]))
                    dve(lambda e, pnxt=pnxt: e.tensor_copy(
                        PTI[:, pnxt + 128:pnxt + 256], PM[:, 768:896]))
                    pcur, pnxt = pnxt, pcur
            tr(kb)
            dve(lambda e: e.tensor_copy(sb["KTM"][:], PMT[:, 0:128]))
            mm(640, sb["KTM"][:], sb["VN"][:, sl])
            dve(lambda e: e.tensor_tensor(
                sb["SF32"][:], sb["SF32"][:], PM[:, 640:768], ALU.add))
            dve(lambda e: e.tensor_copy(sb["SBF"][:], sb["SF32"][:]))

        ts_ = p.op("scalar", lambda e: e.activation(
            sb["SCR2"][:], sb["QF2"][:], AF.Square), inc=s_s)
        p.wait("tensor", s_s, ts_)
        tsync()
        for ti in range(4):
            mt = p.op("tensor", lambda e, ti=ti: e.matmul(
                PL[0:1, ti * 512:(ti + 1) * 512], sb["ONESB"][:, 0:1],
                sb["SCR2"][:, ti * 512:(ti + 1) * 512], start=True, stop=True),
                inc=s_t)
        p.wait("scalar", s_t, mt)
        ts_ = p.op("scalar", lambda e: e.activation(
            RS[0:1, 1024:3072], PL[0:1, 0:2048], AF.Sqrt), inc=s_s)
        p.wait("vector", s_s, ts_)
        tvb = p.op("vector", lambda e: e.tensor_scalar(
            sb["OACC"][0:1, 0:2048], RS[0:1, 1024:3072],
            2.0 * SCALE0, None, ALU.mult), inc=s_v)
        p.wait("tensor", s_v, tvb)
        for tq in range(4):
            tq0 = tq * 512
            tsync()
            mt = p.op("tensor", lambda e, tq0=tq0: e.matmul(
                PL[:, 0:512], sb["ONES1"][0:1, :],
                sb["OACC"][0:1, tq0:tq0 + 512], start=True, stop=True),
                inc=s_t)
            p.wait("vector", s_t, mt)
            tv = p.op("vector", lambda e: e.tensor_copy(
                RS[:, 0:512], PL[:, 0:512]), inc=s_v)
            p.wait("tensor", s_v, tv)
            nck = tq0 // 128 + 4
            for ck in range(nck):
                p.wait("tensor", s_v, s_v.n)
                p.wait("tensor", s_s, s_s.n)
                mt = p.op("tensor", lambda e, ck=ck, tq0=tq0: e.matmul(
                    PM[:, 0:512], sb["KB2"][:, ck * 128:(ck + 1) * 128],
                    sb["QF2"][:, tq0:tq0 + 512], start=True, stop=True),
                    inc=s_t)
                p.wait("vector", s_t, mt)
                tv = p.op("vector", lambda e: e.scalar_tensor_tensor(
                    RS[:, 512:1024], PM[:, 0:512], SCALE0, RS[:, 0:512],
                    ALU.mult, ALU.subtract), inc=s_v)
                p.wait("scalar", s_v, tv)
                ts_ = p.op("scalar", lambda e: e.activation(
                    sb["SCR2"][:, 0:512], RS[:, 512:1024], AF.Exp), inc=s_s)
                p.wait("vector", s_s, ts_)
                r = ck - tq0 // 128
                if r >= 0:
                    tv = p.op("vector", lambda e, r=r: e.tensor_tensor(
                        sb["SCR2"][:, 0:512], sb["SCR2"][:, 0:512],
                        sb["MSK"][:, r * 512:(r + 1) * 512], ALU.mult),
                        inc=s_v)
                    p.wait("tensor", s_v, tv)
                else:
                    p.wait("tensor", s_s, ts_)
                mt = p.op("tensor", lambda e, ck=ck, nck=nck: e.matmul(
                    PL[0:1, 512:1024], sb["ONESB"][:, 0:1],
                    sb["SCR2"][:, 0:512], start=(ck == 0),
                    stop=(ck == nck - 1)), inc=s_t)
                mt = p.op("tensor", lambda e, ck=ck, nck=nck: e.matmul(
                    PL[:, 1536:2048], sb["VN"][:, ck * 128:(ck + 1) * 128],
                    sb["SCR2"][:, 0:512], start=(ck == 0),
                    stop=(ck == nck - 1)), inc=s_t)
                p.wait("vector", s_t, mt)
            tv = p.op("vector", lambda e: e.reciprocal(
                RS[0:1, 3072 - 512:3072], PL[0:1, 512:1024]), inc=s_v)
            p.wait("tensor", s_v, tv)
            mt = p.op("tensor", lambda e: e.matmul(
                PM[:, 512:1024], sb["ONES1"][0:1, :],
                RS[0:1, 3072 - 512:3072], start=True, stop=True), inc=s_t)
            p.wait("vector", s_t, mt)
            tv = p.op("vector", lambda e: e.tensor_copy(
                RS[:, 512:1024], PM[:, 512:1024]), inc=s_v)
            tv = p.op("vector", lambda e, h=h, tq0=tq0: e.tensor_tensor(
                sb["OF"][:, h * 2048 + tq0:h * 2048 + tq0 + 512],
                PL[:, 1536:2048], RS[:, 512:1024], ALU.mult), inc=s_v)
            p.wait("tensor", s_v, tv)

    p.join("a2o", Prog.ENGS)
    war_y = {}
    yp = mk.dram["ypart"]
    for oc in range(KC):
        slot = oc % 2
        if slot in war_w:
            p.wait("gpsimd", s_t, war_w[slot])
        wt = p.op("gpsimd", lambda e, oc=oc, slot=slot: e.dma_start(
            out=sb["WRG"][:, slot * 2048:slot * 2048 + 1024],
            in_=mk.dram["wo2"][oc]), inc=s_w, k=16)
        p.wait("tensor", s_w, wt)
        if "y" in war_y:
            p.wait("tensor", s_v, war_y["y"])
        for ti in range(4):
            for hc in range(8):
                mt = p.op("tensor", lambda e, ti=ti, hc=hc, slot=slot:
                          e.matmul(PL[:, ti * 512:(ti + 1) * 512],
                                   sb["WRG"][:, slot * 2048 + hc * 128:
                                             slot * 2048 + hc * 128 + 128],
                                   sb["OF"][:, hc * 2048 + ti * 512:
                                            hc * 2048 + ti * 512 + 512],
                                   start=(hc == 0), stop=(hc == 7)), inc=s_t)
        war_w[slot] = mt
        p.wait("vector", s_t, mt)
        if "yd" in war_y:
            p.wait("vector", s_x, war_y["yd"])
        war_y["y"] = p.op("vector", lambda e: e.tensor_copy(
            sb["YB"][:], PL[:, 0:2048]), inc=s_v)
        p.wait("sync", s_v, war_y["y"])
        war_y["yd"] = p.op("sync", lambda e, oc=oc: e.dma_start(
            out=yp[oc], in_=sb["YB"][:]), inc=s_x, k=16)
    p.wait("sync", s_x, s_x.n)


def build_attn2_launch():
    mk = MK()
    mk.dram_in("hfull", [KC, 128, 2048], BF16)
    mk.dram_out("ypart", [KC, 128, 2048], F32)
    mk.dram_in("wq2", [8, 128, 2048], F32)
    mk.dram_in("wk2", [8, 128, 2048], F32)
    mk.dram_in("wv2", [8, 128, 2048], F32)
    mk.dram_in("wo2", [16, 128, 1024], F32)
    mk.dram_in("wb2", [128, 128], F32)
    mk.dram_in("inv2", [128, 1], F32)

    alloc_consts(mk)
    mk.sbuf("HB", [128, KC * 2048], BF16)
    mk.sbuf("OF", [128, 8 * 2048], BF16)
    mk.sbuf("QF2", [128, 2048], BF16)
    mk.sbuf("KF2", [128, 2048], BF16)
    mk.sbuf("VF2", [128, 2048], BF16)
    mk.sbuf("KB2", [128, 2048], BF16)
    mk.sbuf("VN", [128, 2048], BF16)
    mk.sbuf("SCR", [128, 1024], BF16)
    mk.sbuf("SCR4", [128, 1024], BF16)
    mk.sbuf("SCR2", [128, 2048], BF16)
    mk.sbuf("SCRI", [128, 1024], I16)
    mk.sbuf("RS", [128, 3072], F32)
    mk.sbuf("OACC", [128, 2048], F32)
    mk.sbuf("RPTC", [128, 2048], BF16)
    mk.sbuf("RPTS", [128, 2048], BF16)
    mk.sbuf("XQ", [128, 1024], BF16)
    mk.sbuf("WRG", [128, 4096], BF16)
    mk.sbuf("YB", [128, 2048], F32)
    mk.sbuf("BETA8", [8, 2048], F32)
    mk.sbuf("BR1", [1, 2048], F32)
    mk.sbuf("IOR", [1, 1024], F32)
    mk.sbuf("ONESR", [1, 512], F32)
    mk.sbuf("TWOS1", [1, 128], F32)
    mk.sbuf("EPS6", [1, 1], F32)
    mk.sbuf("GC", [128, 512], F32)
    mk.sbuf("GR", [128, 512], F32)
    mk.sbuf("MSK", [128, 2048], BF16)
    mk.sbuf("SLM", [128, 128], BF16)
    mk.sbuf("SUM", [128, 128], BF16)
    mk.sbuf("IDB", [128, 128], BF16)
    mk.sbuf("SF32", [128, 128], F32)
    mk.sbuf("SBF", [128, 128], BF16)
    mk.sbuf("PTI", [128, 512], BF16)
    mk.sbuf("RHT", [128, 128], BF16)
    mk.sbuf("RB", [128, 128], BF16)
    mk.sbuf("KTM", [128, 128], BF16)
    mk.sbuf("WB", [128, 128], BF16)
    mk.sbuf("INV5", [128, 1], F32)
    mk.sbuf("SGN", [128, 1], F32)
    mk.sbuf("SHIFT", [128, 1], F32)
    mk.psum("PL", [128, 2048], F32)
    mk.psum("PM", [128, 1536], F32)
    mk.psum("PMT", [128, 1024], BF16)
    emit_attn2(mk)
    return mk.finish()


def build_add_launch():
    mk = MK()
    xin = mk.dram_in("xin", [D, NT], F32)
    ya = mk.dram_in("ya", [D, NT], F32)
    yb = mk.dram_in("yb", [D, NT], F32)
    xout = mk.dram_out("xout", [D, NT], F32)
    mk.sbuf("XA", [128, 2 * 1024], F32)
    mk.sbuf("XB", [128, 2 * 1024], F32)
    mk.sbuf("XC", [128, 2 * 1024], F32)
    p = mk.p
    s_x = p.sem("adx")
    s_v = p.esem("vector")
    xi = xin.rearrange("(c p) t -> c p t", p=128)
    yat = ya.rearrange("(c p) t -> c p t", p=128)
    ybt = yb.rearrange("(c p) t -> c p t", p=128)
    xo = xout.rearrange("(c p) t -> c p t", p=128)
    last = {}
    for c in range(KC):
        par = c % 2
        if par in last:
            p.wait("sync", s_x, last[par])
        t1 = p.op("sync", lambda e, c=c, par=par: e.dma_start(
            out=mk.sb["XA"][:, par * 1024:(par + 1) * 1024], in_=xi[c]),
            inc=s_x, k=16)
        t2 = p.op("sync", lambda e, c=c, par=par: e.dma_start(
            out=mk.sb["XB"][:, par * 1024:(par + 1) * 1024], in_=yat[c]),
            inc=s_x, k=16)
        t3 = p.op("sync", lambda e, c=c, par=par: e.dma_start(
            out=mk.sb["XC"][:, par * 1024:(par + 1) * 1024], in_=ybt[c]),
            inc=s_x, k=16)
        p.wait("vector", s_x, t3)
        p.op("vector", lambda e, par=par: e.tensor_tensor(
            mk.sb["XA"][:, par * 1024:(par + 1) * 1024],
            mk.sb["XA"][:, par * 1024:(par + 1) * 1024],
            mk.sb["XB"][:, par * 1024:(par + 1) * 1024], ALU.add), inc=s_v)
        tv = p.op("vector", lambda e, par=par: e.tensor_tensor(
            mk.sb["XA"][:, par * 1024:(par + 1) * 1024],
            mk.sb["XA"][:, par * 1024:(par + 1) * 1024],
            mk.sb["XC"][:, par * 1024:(par + 1) * 1024], ALU.add), inc=s_v)
        p.wait("sync", s_v, tv)
        last[par] = p.op("sync", lambda e, c=c, par=par: e.dma_start(
            out=xo[c], in_=mk.sb["XA"][:, par * 1024:(par + 1) * 1024]),
            inc=s_x, k=16)
    p.wait("sync", s_x, s_x.n)
    return mk.finish()


def stage_attn0(inp):
    Wq = np.asarray(inp["L0_attn_Wq"], np.float32)
    Wk = np.asarray(inp["L0_attn_Wk"], np.float32)
    Wv = np.asarray(inp["L0_attn_Wv"], np.float32)
    Wo = np.asarray(inp["L0_attn_Wo"], np.float32)
    Wb = np.asarray(inp["L0_attn_Wb"], np.float32)
    inv = (1.0 / 10000.0 ** (np.arange(0, 128, 2, dtype=np.float32) / 128.0))
    inv2 = np.ascontiguousarray(
        np.concatenate([inv, inv]).reshape(128, 1).astype(np.float32))

    def headslab(W, base):
        a = W.reshape(KC, 128, 16, 128)
        return np.ascontiguousarray(
            a[:, :, base:base + 8].transpose(2, 1, 0, 3).reshape(8, 128, 2048))

    per = []
    for half in range(2):
        base = half * 8
        rows = slice(base * 128, base * 128 + 1024)
        wo2 = np.ascontiguousarray(
            Wo[rows].reshape(8, 128, KC, 128).transpose(2, 1, 0, 3)
            .reshape(KC, 128, 1024))
        wb = np.ascontiguousarray(
            Wb[:, base:base + 8].reshape(KC, 128, 8)
            .transpose(1, 0, 2).reshape(128, 128))
        per.append({
            "wq2": headslab(Wq, base),
            "wk2": headslab(Wk, base),
            "wv2": headslab(Wv, base),
            "wo2": wo2,
            "wb2": wb,
            "inv2": inv2,
        })
    return per


def run_attn0(inp, x):
    ncl = _get("lnx", build_ln_launch)
    shards = shard_x(x)
    res = run_bass_kernel_spmd(ncl, [{"xin": s} for s in shards],
                               core_ids=list(range(8)))
    houts = [r["hout"] for r in res.results]
    wa = stage_attn0(inp)
    nc2 = _get("a2", build_attn2_launch)
    in_maps = []
    for c in range(8):
        pc = 2 * (c // 2)
        hfull = np.ascontiguousarray(
            np.concatenate([houts[pc], houts[pc + 1]], axis=2))
        in_maps.append(dict(wa[c % 2], hfull=hfull))
    res = run_bass_kernel_spmd(nc2, in_maps, core_ids=list(range(8)))
    yparts = [r["ypart"].reshape(2048, 2048) for r in res.results]
    nca = _get("add", build_add_launch)
    in_maps = []
    for c in range(8):
        pc = 2 * (c // 2)
        half = c % 2
        sl = slice(half * 1024, (half + 1) * 1024)
        in_maps.append({"xin": shards[c],
                        "ya": np.ascontiguousarray(yparts[pc][:, sl]),
                        "yb": np.ascontiguousarray(yparts[pc + 1][:, sl])})
    res = run_bass_kernel_spmd(nca, in_maps, core_ids=list(range(8)))
    return unshard_x([r["xout"] for r in res.results])



_BUILD_CACHE = {}


def _get(name, fn):
    if name not in _BUILD_CACHE:
        _BUILD_CACHE[name] = fn()
    return _BUILD_CACHE[name]


def kernel(**inputs):
    x = inputs["L0_attn_x"]
    shards = shard_x(x)
    w0 = stage_ffn0(inputs)
    w1 = stage_ffn1(inputs)
    wa0 = stage_attn0(inputs)
    wa1 = stage_attn1(inputs)

    nc = _get("f0", build_ffn0_launch)
    res = run_bass_kernel_spmd(nc, [dict(w0, xin=s) for s in shards],
                               core_ids=list(range(8)))
    shards = [r["xout"] for r in res.results]

    ncl = _get("lnx", build_ln_launch)
    res = run_bass_kernel_spmd(ncl, [{"xin": s} for s in shards],
                               core_ids=list(range(8)))
    houts = [r["hout"] for r in res.results]
    nc2 = _get("a2", build_attn2_launch)
    in_maps = []
    for c in range(8):
        pc = 2 * (c // 2)
        hfull = np.ascontiguousarray(
            np.concatenate([houts[pc], houts[pc + 1]], axis=2))
        in_maps.append(dict(wa0[c % 2], hfull=hfull))
    res = run_bass_kernel_spmd(nc2, in_maps, core_ids=list(range(8)))
    yparts = [r["ypart"].reshape(2048, 2048) for r in res.results]
    nca = _get("add", build_add_launch)
    in_maps = []
    for c in range(8):
        pc = 2 * (c // 2)
        sl = slice((c % 2) * 1024, (c % 2) * 1024 + 1024)
        in_maps.append({"xin": shards[c],
                        "ya": np.ascontiguousarray(yparts[pc][:, sl]),
                        "yb": np.ascontiguousarray(yparts[pc + 1][:, sl])})
    res = run_bass_kernel_spmd(nca, in_maps, core_ids=list(range(8)))
    shards = [r["xout"] for r in res.results]

    nc = _get("f0", build_ffn0_launch)
    res = run_bass_kernel_spmd(nc, [dict(w0, xin=s) for s in shards],
                               core_ids=list(range(8)))
    shards = [r["xout"] for r in res.results]

    nc = _get("f1q", lambda: build_ffn1_launch(True))
    res = run_bass_kernel_spmd(nc, [dict(w1, xin=s) for s in shards],
                               core_ids=list(range(8)))
    caches = [{"w1q": r["w1q"], "w2q": r["w2q"], "rmsv": r["rmsv"]}
              for r in res.results]
    shards = [r["xout"] for r in res.results]

    ncl = _get("lnx", build_ln_launch)
    res = run_bass_kernel_spmd(ncl, [{"xin": s} for s in shards],
                               core_ids=list(range(8)))
    houts = [r["hout"] for r in res.results]
    nc = _get("a5", build_attn5_launch)
    in_maps = []
    for c in range(8):
        pc = 2 * (c // 2)
        hfull = np.ascontiguousarray(
            np.concatenate([houts[pc], houts[pc + 1]], axis=2))
        in_maps.append(dict(wa1[c % 2], xin=shards[c], hfull=hfull))
    res = run_bass_kernel_spmd(nc, in_maps, core_ids=list(range(8)))
    shards = [r["xout"] for r in res.results]

    nc = _get("f1c", lambda: build_ffn1_launch(False))
    res = run_bass_kernel_spmd(
        nc, [dict(caches[c], xin=shards[c]) for c in range(8)],
        core_ids=list(range(8)))
    shards = [r["xout"] for r in res.results]

    return unshard_x(shards)
